# version 01
# baseline (speedup 1.0000x reference)
import contextlib
import os
import numpy as np
import concourse.bass as bass
import concourse.mybir as mybir
from concourse.bass_utils import run_bass_kernel_spmd
from concourse.aot_env import allow_ici_all_to_all

F32 = mybir.dt.float32
BF16 = mybir.dt.bfloat16
AF = mybir.ActivationFunctionType
OP = mybir.AluOpType

NC = 8
L = 16384
D = 2048
H = 8192
NST = 16
RA = 64
RB = 128
TBLK = 512
SS = 2048
NSEG = L // SS
NCH = 4
DL = D // NC
NDC = 16
LT = L // NC
EPS = 1e-5
NHC = H // 128


class Em:

    ENGINES = ("tensor", "vector", "scalar", "gpsimd", "sync")

    def __init__(self, nc, ctx):
        self.nc = nc
        self.streams = {e: [] for e in self.ENGINES}
        self.sems = {e: ctx.enter_context(nc.semaphore(f"m_{e}"))
                     for e in self.ENGINES}
        self.counts = {e: 0 for e in self.ENGINES}
        self.ctx = ctx
        self.dsems = {}
        self.ccsems = []
        self.bw = {}
        self.br = {}

    def _sem_of(self, tag):
        if tag in self.sems:
            return self.sems[tag]
        if isinstance(tag, tuple) and tag[0] == "dk":
            return self.dsems[tag[1]][0]
        if isinstance(tag, tuple) and tag[0] == "cc":
            return self.ccsems[tag[1]]
        raise KeyError(tag)

    def _emit_waits(self, stream_eng, waits):
        noself = os.environ.get("K_SELF") != "1"
        best = {}
        for tag, count in waits:
            if count <= 0 or (noself and tag == stream_eng):
                continue
            if best.get(tag, 0) < count:
                best[tag] = count
        for tag, count in best.items():
            sem = self._sem_of(tag)
            self.streams[stream_eng].append(
                lambda e, s=sem, c=count: e.wait_ge(s, c))

    @staticmethod
    def _is_psum_key(k):
        return (k in ("pscL", "pscH", "pg", "pu")
                or (isinstance(k, tuple) and k and k[0] == "pd"))

    def _dep_waits(self, reads, writes):
        waits = []
        for k in reads:
            if k in self.bw:
                waits.append(self.bw[k])
            if self._is_psum_key(k):
                waits.extend(self.br.get(k, {}).items())
        for k in writes:
            if k in self.bw:
                waits.append(self.bw[k])
            waits.extend(self.br.get(k, {}).items())
        return waits

    def _record(self, tag, count, reads, writes):
        for k in writes:
            self.bw[k] = (tag, count)
            self.br[k] = {}
        for k in reads:
            self.br.setdefault(k, {})[tag] = count

    def op(self, eng, fn, reads=(), writes=()):
        self._emit_waits(eng, self._dep_waits(reads, writes))
        self.counts[eng] += 1
        cnt, sem = self.counts[eng], self.sems[eng]

        def run(e, fn=fn, sem=sem):
            fn(e).then_inc(sem, 1)
        self.streams[eng].append(run)
        self._record(eng, cnt, reads, writes)
        return (eng, cnt)

    def mm(self, fns, reads=(), writes=()):
        self._emit_waits("tensor", self._dep_waits(reads, writes))
        self.counts["tensor"] += 1
        cnt, sem = self.counts["tensor"], self.sems["tensor"]

        def run(e, fns=fns, sem=sem):
            last = None
            for f in fns:
                last = f(e)
            last.then_inc(sem, 1)
        self.streams["tensor"].append(run)
        self._record("tensor", cnt, reads, writes)
        return ("tensor", cnt)

    def dma(self, queue, out, in_, reads=(), writes=()):
        self._emit_waits(queue, self._dep_waits(reads, writes))
        name = out.tensor.name
        if name not in self.dsems:
            self.dsems[name] = [
                self.ctx.enter_context(self.nc.semaphore(f"dk_{name}")), 0]
        ent = self.dsems[name]
        ent[1] += 16
        cnt, sem = ent[1], ent[0]

        def run(e, out=out, in_=in_, sem=sem):
            e.dma_start(out=out, in_=in_).then_inc(sem, 16)
        self.streams[queue].append(run)
        self._record(("dk", name), cnt, reads, writes)
        return (("dk", name), cnt)

    def coll(self, kind, op, in_ap, out_ap, reads=(), writes=()):
        self._emit_waits("gpsimd", self._dep_waits(reads, writes))
        idx = len(self.ccsems)
        sem = self.ctx.enter_context(self.nc.semaphore(f"cc_{idx}"))
        self.ccsems.append(sem)

        def run(e, kind=kind, op=op, i=in_ap, o=out_ap, sem=sem):
            with allow_ici_all_to_all():
                e.collective_compute(
                    kind, op, replica_groups=[list(range(NC))],
                    ins=[i], outs=[o]).then_inc(sem)
        self.streams["gpsimd"].append(run)
        self._record(("cc", idx), 1, reads, writes)
        return (("cc", idx), 1)

    def finish(self, block):
        for eng in self.ENGINES:
            stream = self.streams[eng]

            def body(e, stream=stream):
                for f in stream:
                    f(e)
            getattr(block, eng)(body)


def build_nc(stage=4):
    nc = bass.Bass("TRN2", target_bir_lowering=False)

    def ein(name, shape, dt=F32):
        return nc.dram_tensor(name, shape, dt, kind="ExternalInput").ap()

    def idr(name, shape, dt=F32):
        return nc.dram_tensor(name, shape, dt).ap()

    x0_tok = ein("x0_tok", [D, LT])
    x0_dsh = ein("x0_dsh", [DL, L])
    wxa1 = ein("wxa1", [D, RA])
    wxa2 = ein("wxa2", [D, 48])
    wxb1 = ein("wxb1", [DL, RB])
    wxb2 = ein("wxb2", [DL, 48])
    wdta = ein("wdta", [RA, DL])
    wdtb = ein("wdtb", [RB, DL])
    bdta = ein("bdta", [128, 2])
    bdtb = ein("bdtb", [128, 2])
    aloga = ein("aloga", [128, 2 * NST])
    alogb = ein("alogb", [128, 2 * NST])
    dpar = ein("dpar", [128, 2])
    dskp = ein("dskp", [128, 2])
    cwin = ein("cwin", [128, 8])
    cbin = ein("cbin", [128, 2])
    selin = ein("selin", [NST, NST * 128])
    onesin = ein("onesin", [1, 128])
    wg_s = ein("wg_s", [NHC, 128, 2048])
    wu_s = ein("wu_s", [NHC, 128, 2048])
    wd_s = ein("wd_s", [NDC, 128, H])

    y_out = nc.dram_tensor("y_out", [D, LT], F32, kind="ExternalOutput").ap()

    wg_b = idr("wg_b", [NHC, 128, 2048], BF16)
    wu_b = idr("wu_b", [NHC, 128, 2048], BF16)
    wd_b = idr("wd_b", [NDC, 128, H], BF16)
    gA_src = [idr(f"gA_src{g}", [98, TBLK]) for g in range(NCH)]
    gA_dst = [idr(f"gA_dst{g}", [NC * 98, TBLK]) for g in range(NCH)]
    y1_src = [idr(f"y1_src{g}", [D, TBLK]) for g in range(NCH)]
    y1_dst = [idr(f"y1_dst{g}", [D, TBLK]) for g in range(NCH)]
    x2_src = [idr(f"x2_src{g}", [D, TBLK]) for g in range(NCH)]
    x2_dst = [idr(f"x2_dst{g}", [D, TBLK]) for g in range(NCH)]
    y3_src = [idr(f"y3_src{g}", [D, TBLK]) for g in range(NCH)]
    y3_dst = [idr(f"y3_dst{g}", [D, TBLK]) for g in range(NCH)]
    s3_src = [idr(f"s3_src{g}", [2, TBLK]) for g in range(NCH)]
    s3_dst = [idr(f"s3_dst{g}", [NC * 2, TBLK]) for g in range(NCH)]
    ar_src = [idr(f"ar_src{s}", [176, SS]) for s in range(NSEG)]
    ar_dst = [idr(f"ar_dst{s}", [176, SS]) for s in range(NSEG)]
    x1_stash = [idr(f"x1_stash{g}", [D, TBLK]) for g in range(NCH)]
    x3_stash = [idr(f"x3_stash{g}", [D, TBLK]) for g in range(NCH)]
    g_stash = [idr(f"g_stash{p}", [128, NHC * TBLK], BF16) for p in range(2)]

    with contextlib.ExitStack() as ctx:
        def sb(n, s, d=F32):
            return ctx.enter_context(nc.sbuf_tensor(n, s, d))

        em = Em(nc, ctx)

        t_wxa1 = sb("t_wxa1", [128, NDC * RA], BF16)
        t_wxa2 = sb("t_wxa2", [128, NDC * 48], BF16)
        t_wxb1 = sb("t_wxb1", [128, 2 * RB], BF16)
        t_wxb2 = sb("t_wxb2", [128, 2 * 48], BF16)
        t_wdta = sb("t_wdta", [RA, DL], BF16)
        t_wdtb = sb("t_wdtb", [RB, DL], BF16)
        t_sel = sb("t_sel", [NST, NST * 128], BF16)
        t_ones = sb("t_ones", [33, 128])
        t_onesb = sb("t_onesb", [1, 128], BF16)
        t_onescol = sb("t_onescol", [128, 1], BF16)
        t_bdta = sb("t_bdta", [128, 2])
        t_bdtb = sb("t_bdtb", [128, 2])
        t_anegA = sb("t_anegA", [128, 2 * NST])
        t_anegB = sb("t_anegB", [128, 2 * NST])
        t_dpar = sb("t_dpar", [128, 2])
        t_dskp = sb("t_dskp", [128, 2])
        t_cw = sb("t_cw", [128, 8])
        t_cb = sb("t_cb", [128, 2])
        t_wsumA = sb("t_wsumA", [112, 1])
        t_carA = [sb(f"t_carA{c}", [128, NST]) for c in range(2)]
        t_carB = [sb(f"t_carB{c}", [128, NST]) for c in range(2)]
        t_tail = [sb(f"t_tail{c}", [128, 4]) for c in range(2)]
        t_st1 = sb("t_st1", [33, SS])
        t_st3 = t_st1
        t_srow = sb("t_srow", [33, TBLK])
        t_svar = sb("t_svar", [33, TBLK])

        t_zz = sb("t_zz", [128, SS + 4])
        t_z = [t_zz, t_zz]
        t_u = [sb(f"t_u{c}", [128, SS]) for c in range(2)]
        t_ub = [sb(f"t_ub{c}", [128, SS], BF16) for c in range(2)]
        t_dt = [sb(f"t_dt{c}", [128, SS]) for c in range(2)]
        t_dx = [sb(f"t_dx{c}", [128, SS]) for c in range(2)]
        t_y = [sb(f"t_y{c}", [128, SS]) for c in range(2)]
        t_sp = sb("t_sp", [128, SS])
        t_b = [sb(f"t_b{c}", [128, SS]) for c in range(2)]
        t_h = sb("t_h", [128, SS])
        t_cbc = sb("t_cbc", [128, SS])
        t_dtrBb = sb("t_dtrBb", [RB, SS], BF16)
        t_dtrAb = t_dtrBb[0:RA, :]
        t_bbb = sb("t_bbb", [NST, SS], BF16)
        t_crb = sb("t_crb", [NST, SS], BF16)

        t_x = sb("t_x", [128, TBLK])
        t_xa = sb("t_xa", [128, TBLK])
        t_y1r = sb("t_y1r", [128, TBLK])
        t_xb16 = sb("t_xb16", [128, 2 * TBLK], BF16)
        t_z2 = sb("t_z2", [128, NDC * TBLK], BF16)
        t_sg = sb("t_sg", [128, TBLK])
        t_su = sb("t_su", [128, TBLK])
        t_g = sb("t_g", [128, TBLK], BF16)
        t_gs = [sb(f"t_gs{p}", [128, 4 * TBLK], BF16) for p in range(2)]
        t_wgu = [sb(f"t_wgu{p}", [128, 4096], BF16) for p in range(2)]
        t_wd = t_wgu
        t_xdb = t_sp[0:112, 0:TBLK]

        PSC = ctx.enter_context(nc.psum_tensor("PSC", [128, 2048], F32))
        PF = ctx.enter_context(nc.psum_tensor("PF", [128, 2048], F32))
        PG = PF[:, 0:512]
        PU = PF[:, 512:1024]
        PD = [PF[:, 1024:1536], PF[:, 1536:2048]]


        em.dma("sync", t_ones[0:1, :], onesin, writes=["ones"])
        em.dma("sync", t_ones[32:33, :], onesin, writes=["ones"])
        em.op("scalar", lambda e: e.copy(out=t_onesb[:, :], in_=t_ones[0:1, :]),
              reads=["ones"], writes=["onesb"])
        em.op("vector", lambda e: e.memset(t_onescol[:, :], 1.0),
              writes=["onescol"])
        for c in range(2):
            em.op("vector", lambda e, c=c: e.memset(t_carA[c][:, :], 0.0),
                  writes=[("carA", c)])
            em.op("vector", lambda e, c=c: e.memset(t_carB[c][:, :], 0.0),
                  writes=[("carB", c)])
            em.op("vector", lambda e, c=c: e.memset(t_tail[c][:, :], 0.0),
                  writes=[("tail", c)])

        def load_cast(key, dst, src, rows, cols):
            em.dma("sync", t_sp[0:rows, 0:cols], src, writes=["sp"])
            em.op("scalar", lambda e, r=rows, co=cols, d=dst: e.copy(
                out=d, in_=t_sp[0:r, 0:co]),
                reads=["sp"], writes=[key])

        for dc in range(NDC):
            load_cast(("wxa1", dc), t_wxa1[:, RA * dc:RA * (dc + 1)],
                      wxa1[128 * dc:128 * (dc + 1), :], 128, RA)
            load_cast(("wxa2", dc), t_wxa2[:, 48 * dc:48 * (dc + 1)],
                      wxa2[128 * dc:128 * (dc + 1), :], 128, 48)
        for c in range(2):
            load_cast(("wxb1", c), t_wxb1[:, RB * c:RB * (c + 1)],
                      wxb1[128 * c:128 * (c + 1), :], 128, RB)
            load_cast(("wxb2", c), t_wxb2[:, 48 * c:48 * (c + 1)],
                      wxb2[128 * c:128 * (c + 1), :], 128, 48)
        load_cast("wdta", t_wdta[:, :], wdta, RA, DL)
        load_cast("wdtb", t_wdtb[:, :], wdtb, RB, DL)
        load_cast("sel", t_sel[:, :], selin, NST, NST * 128)
        em.dma("sync", t_bdta[:, :], bdta, writes=["bdta"])
        em.dma("sync", t_bdtb[:, :], bdtb, writes=["bdtb"])
        em.dma("sync", t_dpar[:, :], dpar, writes=["dpar"])
        em.dma("sync", t_dskp[:, :], dskp, writes=["dskp"])
        em.dma("sync", t_cw[:, :], cwin, writes=["cw"])
        em.dma("sync", t_cb[:, :], cbin, writes=["cb"])
        em.dma("sync", t_anegA[:, :], aloga, writes=["anegA"])
        em.dma("sync", t_anegB[:, :], alogb, writes=["anegB"])
        for t_an, key in ((t_anegA, "anegA"), (t_anegB, "anegB")):
            em.op("scalar", lambda e, t=t_an: e.activation(
                out=t[:, :], in_=t[:, :], func=AF.Exp),
                reads=[key], writes=[key])
            em.op("vector", lambda e, t=t_an: e.tensor_scalar(
                out=t[:, :], in0=t[:, :], scalar1=-1.0, scalar2=None,
                op0=OP.mult), reads=[key], writes=[key])

        em.mm([(lambda e, dc=dc: e.matmul(
            PSC[0:RA, 0:1], t_wxa1[:, RA * dc:RA * (dc + 1)],
            t_onescol[:, :], start=(dc == 0), stop=(dc == NDC - 1)))
            for dc in range(NDC)],
            reads=[("wxa1", dc) for dc in range(NDC)] + ["onescol"],
            writes=["pscL"])
        em.op("scalar", lambda e: e.copy(out=t_wsumA[0:RA, :],
                                         in_=PSC[0:RA, 0:1]),
              reads=["pscL"], writes=["wsumA"])
        em.mm([(lambda e, dc=dc: e.matmul(
            PSC[0:48, 0:1], t_wxa2[:, 48 * dc:48 * (dc + 1)],
            t_onescol[:, :], start=(dc == 0), stop=(dc == NDC - 1)))
            for dc in range(NDC)],
            reads=[("wxa2", dc) for dc in range(NDC)] + ["onescol"],
            writes=["pscL"])
        em.op("scalar", lambda e: e.copy(out=t_wsumA[64:112, :],
                                         in_=PSC[0:48, 0:1]),
              reads=["pscL"], writes=["wsumA"])

        if stage >= 2:
            def convert(src, dst, nslab, npiece):
                for i in range(nslab):
                    for p in range(npiece):
                        cs = slice(2048 * p, 2048 * (p + 1))
                        em.dma("sync", t_sp[:, :], src[i][:, cs],
                               writes=["sp"])
                        em.op("scalar", lambda e: e.copy(
                            out=t_ub[0][:, :], in_=t_sp[:, :]),
                            reads=["sp"], writes=[("ub", 0)])
                        em.dma("sync", dst[i][:, cs], t_ub[0][:, :],
                               reads=[("ub", 0)])
            convert(wg_s, wg_b, NHC, 1)
            convert(wu_s, wu_b, NHC, 1)
            convert(wd_s, wd_b, NDC, 4)

        def ln_finalize(srow_key):
            em.op("vector", lambda e: e.tensor_scalar(
                out=t_srow[0:1, :], in0=PG[0:1, :], scalar1=-1.0 / D,
                scalar2=None, op0=OP.mult), reads=["pg"], writes=[srow_key])
            em.op("scalar", lambda e: e.activation(
                out=t_svar[0:1, :], in_=t_srow[0:1, :],
                func=AF.Square), reads=[srow_key], writes=["svar"])
            em.op("vector", lambda e: e.scalar_tensor_tensor(
                out=t_svar[32:33, :], in0=PU[0:1, :],
                scalar=1.0 / D, in1=t_svar[0:1, :],
                op0=OP.mult, op1=OP.subtract),
                reads=["pu", "svar"], writes=["svar"])
            em.op("vector", lambda e: e.tensor_scalar(
                out=t_svar[32:33, :], in0=t_svar[32:33, :], scalar1=EPS,
                scalar2=None, op0=OP.add), reads=["svar"], writes=["svar"])
            em.op("scalar", lambda e: e.activation(
                out=t_svar[32:33, :], in_=t_svar[32:33, :],
                func=AF.Ln), reads=["svar"], writes=["svar"])
            em.op("scalar", lambda e: e.activation(
                out=t_srow[32:33, :], in_=t_svar[32:33, :],
                func=AF.Exp, scale=-0.5), reads=["svar"], writes=[srow_key])

        def stat_mms(dc, xkeys):
            em.op("scalar", lambda e: e.copy(
                out=t_xb16[:, 0:TBLK], in_=t_x[:, :]),
                reads=xkeys, writes=["xb16"])
            em.op("scalar", lambda e: e.activation(
                out=t_xb16[:, TBLK:2 * TBLK], in_=t_x[:, :], func=AF.Square),
                reads=xkeys, writes=["xb16"])
            em.mm([lambda e, dc=dc: e.matmul(
                PG[0:1, :], t_onescol[:, :], t_xb16[:, 0:TBLK],
                start=(dc == 0), stop=(dc == NDC - 1))],
                reads=["xb16", "onescol"], writes=["pg"])
            em.mm([lambda e, dc=dc: e.matmul(
                PU[0:1, :], t_onescol[:, :], t_xb16[:, TBLK:2 * TBLK],
                start=(dc == 0), stop=(dc == NDC - 1))],
                reads=["xb16", "onescol"], writes=["pu"])

        _pre = int(os.environ.get("K_PRE", str(NCH)))
        for g in range(_pre):
            tb = slice(TBLK * g, TBLK * (g + 1))
            for dc in range(NDC):
                em.dma("sync", t_x[:, :],
                       x0_tok[128 * dc:128 * (dc + 1), tb], writes=["tx"])
                stat_mms(dc, ["tx"])
                em.mm([lambda e, dc=dc: e.matmul(
                    PSC[0:RA, 0:TBLK], t_wxa1[:, RA * dc:RA * (dc + 1)],
                    t_xb16[:, 0:TBLK], start=(dc == 0), stop=(dc == NDC - 1))],
                    reads=["xb16", ("wxa1", dc)], writes=["pscL"])
                em.mm([lambda e, dc=dc: e.matmul(
                    PSC[0:48, TBLK:2 * TBLK], t_wxa2[:, 48 * dc:48 * (dc + 1)],
                    t_xb16[:, 0:TBLK], start=(dc == 0), stop=(dc == NDC - 1))],
                    reads=["xb16", ("wxa2", dc)], writes=["pscL"])
            ln_finalize("srow")
            em.mm([lambda e: e.matmul(PD[0][:, :], t_ones[0:1, :],
                                      t_srow[0:1, :],
                                      start=True, stop=True)],
                  reads=["srow", "ones"], writes=[("pd", 0)])
            em.mm([lambda e: e.matmul(PD[1][:, :], t_ones[32:33, :],
                                      t_srow[32:33, :],
                                      start=True, stop=True)],
                  reads=["srow", "ones"], writes=[("pd", 1)])
            em.op("scalar", lambda e: e.copy(out=t_xdb[0:RA, :],
                                             in_=PSC[0:RA, 0:TBLK]),
                  reads=["pscL"], writes=["sp"])
            em.op("scalar", lambda e: e.copy(out=t_xdb[64:112, :],
                                             in_=PSC[0:48, TBLK:2 * TBLK]),
                  reads=["pscL"], writes=["sp"])
            em.op("vector", lambda e: e.scalar_tensor_tensor(
                out=t_xdb[0:112, :], in0=PD[0][0:112, :],
                scalar=t_wsumA[0:112, :], in1=t_xdb[0:112, :],
                op0=OP.mult, op1=OP.add),
                reads=[("pd", 0), "wsumA", "sp"], writes=["sp"])
            em.op("vector", lambda e: e.tensor_tensor(
                out=t_xdb[0:112, :], in0=t_xdb[0:112, :],
                in1=PD[1][0:112, :], op=OP.mult),
                reads=["sp", ("pd", 1)], writes=["sp"])
            em.dma("gpsimd", gA_src[g][0:64, :], t_xdb[0:64, :],
                   reads=["sp"], writes=[("gsrc", g)])
            em.dma("gpsimd", gA_src[g][64:80, :], t_xdb[64:80, :],
                   reads=["sp"], writes=[("gsrc", g)])
            em.dma("gpsimd", gA_src[g][80:96, :], t_xdb[96:112, :],
                   reads=["sp"], writes=[("gsrc", g)])
            em.dma("gpsimd", gA_src[g][96:97, :], t_srow[0:1, :],
                   reads=["srow"], writes=[("gsrc", g)])
            em.dma("gpsimd", gA_src[g][97:98, :], t_srow[32:33, :],
                   reads=["srow"], writes=[("gsrc", g)])
            em.coll("AllGather", OP.bypass, gA_src[g], gA_dst[g],
                    reads=[("gsrc", g)], writes=[("gA", g)])

        def ln_apply_seg(dc, dst_ap, src_ap, src_key, st_tile, st_key,
                         dst_key):
            for h2 in range(2):
                cs = slice(1024 * h2, 1024 * (h2 + 1))
                em.mm([(lambda e, q=q, h2=h2: e.matmul(
                    PSC[:, 512 * q:512 * (q + 1)], t_ones[0:1, :],
                    st_tile[0:1,
                            1024 * h2 + 512 * q:1024 * h2 + 512 * (q + 1)],
                    start=True, stop=True)) for q in range(2)],
                    reads=[st_key, "ones"], writes=["pscL"])
                em.mm([(lambda e, q=q, h2=h2: e.matmul(
                    PSC[:, 1024 + 512 * q:1024 + 512 * (q + 1)],
                    t_ones[32:33, :],
                    st_tile[32:33,
                            1024 * h2 + 512 * q:1024 * h2 + 512 * (q + 1)],
                    start=True, stop=True)) for q in range(2)],
                    reads=[st_key, "ones"], writes=["pscH"])
                em.op("vector", lambda e, cs=cs: e.tensor_tensor(
                    out=t_sp[:, cs], in0=src_ap[:, cs],
                    in1=PSC[:, 0:1024], op=OP.add),
                    reads=[src_key, "pscL"], writes=["sp"])
                em.op("vector", lambda e, cs=cs: e.tensor_tensor(
                    out=dst_ap[:, cs], in0=t_sp[:, cs],
                    in1=PSC[:, 1024:2048], op=OP.mult),
                    reads=["sp", "pscH"], writes=[dst_key])

        def softplus(dc, tbias, bias_key, dt_key):
            em.op("vector", lambda e, dc=dc: e.tensor_scalar(
                out=t_dt[dc][:, :], in0=PSC[:, :],
                scalar1=tbias[:, dc:dc + 1], scalar2=0.0,
                op0=OP.add, op1=OP.max),
                reads=["pscL", "pscH", bias_key], writes=[dt_key])
            em.op("scalar", lambda e, dc=dc: e.activation(
                out=t_sp[:, :], in_=PSC[:, :], func=AF.Abs,
                bias=tbias[:, dc:dc + 1]),
                reads=["pscL", "pscH", bias_key], writes=["sp"])
            em.op("scalar", lambda e: e.activation(
                out=t_sp[:, :], in_=t_sp[:, :], func=AF.Exp,
                scale=-1.0), reads=["sp"], writes=["sp"])
            em.op("scalar", lambda e: e.activation(
                out=t_sp[:, :], in_=t_sp[:, :], func=AF.Ln,
                bias=1.0), reads=["sp"], writes=["sp"])
            em.op("vector", lambda e, dc=dc: e.tensor_tensor(
                out=t_dt[dc][:, :], in0=t_dt[dc][:, :], in1=t_sp[:, :],
                op=OP.add), reads=[dt_key, "sp"], writes=[dt_key])

        def scan_sublayer(scan_id, s):
            _sub = int(os.environ.get("K_SCANSUB", "5"))
            g, half = s // 2, s % 2
            if scan_id == 0:
                t_dtrb = t_dtrAb
                wdt_t, wdt_key = t_wdta, "wdta"
                tbias, bias_key = t_bdta, "bdta"
                t_aneg, aneg_key = t_anegA, "anegA"
                t_car, car_key = t_carA, "carA"
                t_skip, skip_key = t_dpar, "dpar"
            else:
                t_dtrb = t_dtrBb
                wdt_t, wdt_key = t_wdtb, "wdtb"
                tbias, bias_key = t_bdtb, "bdtb"
                t_aneg, aneg_key = t_anegB, "anegB"
                t_car, car_key = t_carB, "carB"
                t_skip, skip_key = t_dskp, "dskp"

            if scan_id == 0:
                for k in range(4):
                    i = 4 * half + k
                    base = 98 * i
                    cseg = slice(512 * k, 512 * (k + 1))
                    em.dma("gpsimd", t_dtrb[:, cseg],
                           gA_dst[g][base:base + 64, :],
                           reads=[("gA", g)], writes=["dtrb"])
                    em.dma("gpsimd", t_bbb[:, cseg],
                           gA_dst[g][base + 64:base + 80, :],
                           reads=[("gA", g)], writes=["bbb"])
                    em.dma("gpsimd", t_crb[:, cseg],
                           gA_dst[g][base + 80:base + 96, :],
                           reads=[("gA", g)], writes=["crb"])
                    em.dma("sync", t_st1[0:1, cseg],
                           gA_dst[g][base + 96:base + 97, :],
                           reads=[("gA", g)], writes=["st1"])
                    em.dma("sync",
                           t_st1[32:33, 512 * k:512 * (k + 1)],
                           gA_dst[g][base + 97:base + 98, :],
                           reads=[("gA", g)], writes=["st1"])
            else:
                em.dma("gpsimd", t_dtrb[:, :], ar_dst[s][0:RB, :],
                       reads=[("ar", s)], writes=["dtrb"])
                em.dma("gpsimd", t_bbb[:, :], ar_dst[s][128:144, :],
                       reads=[("ar", s)], writes=["bbb"])
                em.dma("gpsimd", t_crb[:, :], ar_dst[s][144:160, :],
                       reads=[("ar", s)], writes=["crb"])

            if _sub < 2:
                return
            if scan_id == 0:
                for dc in range(2):
                    em.dma("sync", t_u[dc][:, :],
                           x0_dsh[128 * dc:128 * (dc + 1),
                                  SS * s:SS * (s + 1)],
                           writes=[("u", dc)])
                    ln_apply_seg(dc, t_u[dc], t_u[dc], ("u", dc),
                                 t_st1, "st1", ("u", dc))

            if _sub < 3:
                return
            _dtp = int(os.environ.get("K_DTPART", "3"))
            for dc in range(2):
                em.mm([(lambda e, q=q, dc=dc: e.matmul(
                    PSC[:, 512 * q:512 * (q + 1)],
                    wdt_t[:, 128 * dc:128 * (dc + 1)],
                    t_dtrb[:, 512 * q:512 * (q + 1)],
                    start=True, stop=True)) for q in range(4)],
                    reads=["dtrb", wdt_key], writes=["pscL", "pscH"])
                if _dtp == 4:
                    em.op("vector", lambda e, dc=dc: e.tensor_scalar(
                        out=t_dt[dc][:, :], in0=PSC[:, :],
                        scalar1=tbias[:, dc:dc + 1], scalar2=0.0,
                        op0=OP.add, op1=OP.max),
                        reads=["pscL", "pscH", bias_key], writes=[("dt", dc)])
                elif _dtp == 5:
                    em.op("scalar", lambda e, dc=dc: e.activation(
                        out=t_dt[dc][:, :], in_=PSC[:, :], func=AF.Abs,
                        bias=tbias[:, dc:dc + 1]),
                        reads=["pscL", "pscH", bias_key], writes=[("dt", dc)])
                elif _dtp >= 2:
                    softplus(dc, tbias, bias_key, ("dt", dc))
                else:
                    em.op("vector", lambda e, dc=dc: e.tensor_copy(
                        out=t_dt[dc][:, :], in_=PSC[:, :]),
                        reads=["pscL", "pscH"], writes=[("dt", dc)])
                if _dtp >= 3:
                    em.op("vector", lambda e, dc=dc: e.tensor_tensor(
                        out=t_dx[dc][:, :], in0=t_dt[dc][:, :],
                        in1=t_u[dc][:, :], op=OP.mult),
                        reads=[("dt", dc), ("u", dc)], writes=[("dx", dc)])
                    em.op("vector", lambda e, dc=dc: e.tensor_scalar(
                        out=t_y[dc][:, :], in0=t_u[dc][:, :],
                        scalar1=t_skip[:, dc:dc + 1], scalar2=None,
                        op0=OP.mult),
                        reads=[("u", dc), skip_key], writes=[("y", dc)])

            if _sub < 4:
                return
            for n in range(NST):
                for h2 in range(2):
                    cs = slice(1024 * h2, 1024 * (h2 + 1))
                    em.mm([(lambda e, q=q, n=n, h2=h2: e.matmul(
                        PSC[:, 512 * q:512 * (q + 1)],
                        t_sel[:, 128 * n:128 * (n + 1)],
                        t_bbb[:, 1024 * h2 + 512 * q:
                              1024 * h2 + 512 * (q + 1)],
                        start=True, stop=True)) for q in range(2)],
                        reads=["bbb", "sel"], writes=["pscL"])
                    em.mm([(lambda e, q=q, n=n, h2=h2: e.matmul(
                        PSC[:, 1024 + 512 * q:1024 + 512 * (q + 1)],
                        t_sel[:, 128 * n:128 * (n + 1)],
                        t_crb[:, 1024 * h2 + 512 * q:
                              1024 * h2 + 512 * (q + 1)],
                        start=True, stop=True)) for q in range(2)],
                        reads=["crb", "sel"], writes=["pscH"])
                    em.op("scalar", lambda e, cs=cs: e.copy(
                        out=t_cbc[:, cs], in_=PSC[:, 1024:2048]),
                        reads=["pscH"], writes=["cbc"])
                    for dc in range(2):
                        em.op("vector", lambda e, dc=dc, cs=cs:
                              e.tensor_tensor(
                                  out=t_b[dc][:, cs], in0=t_dx[dc][:, cs],
                                  in1=PSC[:, 0:1024], op=OP.mult),
                              reads=[("dx", dc), "pscL"], writes=[("b", dc)])
                cbc_key = "cbc"
                for dc in range(2):
                    em.op("scalar", lambda e, dc=dc, n=n: e.activation(
                        out=t_sp[:, :], in_=t_dt[dc][:, :], func=AF.Exp,
                        scale=t_aneg[:, NST * dc + n:NST * dc + n + 1]),
                        reads=[("dt", dc), aneg_key], writes=["sp"])
                    em.op("vector", lambda e, dc=dc, n=n: e.tensor_tensor_scan(
                        out=t_h[:, :], data0=t_sp[:, :],
                        data1=t_b[dc][:, :],
                        initial=t_car[dc][:, n:n + 1],
                        op0=OP.mult, op1=OP.add),
                        reads=["sp", ("b", dc), (car_key, dc)],
                        writes=["h"])
                    em.op("scalar", lambda e, dc=dc, n=n: e.copy(
                        out=t_car[dc][:, n:n + 1], in_=t_h[:, SS - 1:SS]),
                        reads=["h"], writes=[(car_key, dc)])
                    em.op("vector", lambda e, dc=dc: e.tensor_tensor(
                        out=t_b[dc][:, :], in0=t_h[:, :],
                        in1=t_cbc[:, :], op=OP.mult),
                        reads=["h", cbc_key], writes=[("b", dc)])
                    em.op("vector", lambda e, dc=dc: e.tensor_tensor(
                        out=t_y[dc][:, :], in0=t_y[dc][:, :],
                        in1=t_b[dc][:, :], op=OP.add),
                        reads=[("y", dc), ("b", dc)], writes=[("y", dc)])

            if _sub < 5:
                return
            ysrc = y1_src if scan_id == 0 else y3_src
            for dc in range(2):
                for k in range(4):
                    i = 4 * half + k
                    em.dma("gpsimd",
                           ysrc[g][256 * i + 128 * dc:
                                   256 * i + 128 * (dc + 1), :],
                           t_y[dc][:, 512 * k:512 * (k + 1)],
                           reads=[("y", dc)], writes=[("ysrc", scan_id, g)])
            if half == 1:
                if scan_id == 0:
                    em.coll("AllToAll", OP.bypass, y1_src[g], y1_dst[g],
                            reads=[("ysrc", 0, g)], writes=[("y1d", g)])
                else:
                    em.coll("AllToAll", OP.bypass, y3_src[g], y3_dst[g],
                            reads=[("ysrc", 1, g)], writes=[("y3d", g)])

        def scanB_part1(s):
            g, half = s // 2, s % 2
            for k in range(4):
                i = 4 * half + k
                em.dma("sync", t_st3[0:1, 512 * k:512 * (k + 1)],
                       s3_dst[g][2 * i:2 * i + 1, :],
                       reads=[("s3d", g)], writes=["st1"])
                em.dma("sync", t_st3[32:33, 512 * k:512 * (k + 1)],
                       s3_dst[g][2 * i + 1:2 * i + 2, :],
                       reads=[("s3d", g)], writes=["st1"])
            for dc in range(2):
                src = x2_dst[g].rearrange("(i d) t -> d i t", i=NC)[
                    128 * dc:128 * (dc + 1), 4 * half:4 * half + 4, :]
                em.dma("sync",
                       t_z[dc][:, 4:4 + SS].rearrange(
                           "p (k t) -> p k t", k=4),
                       src, reads=[("x2d", g)], writes=["zt"])
                em.op("scalar", lambda e, dc=dc: e.copy(
                    out=t_z[dc][:, 1:4], in_=t_tail[dc][:, 1:4]),
                    reads=[("tail", dc)], writes=["zt"])
                ln_apply_seg(dc, t_z[dc][:, 4:4 + SS], t_z[dc][:, 4:4 + SS],
                             "zt", t_st3, "st1", "zt")
                em.op("scalar", lambda e, dc=dc: e.copy(
                    out=t_tail[dc][:, 1:4], in_=t_z[dc][:, 1 + SS:4 + SS]),
                    reads=["zt"], writes=[("tail", dc)])
                em.op("vector", lambda e, dc=dc: e.tensor_scalar(
                    out=t_u[dc][:, :], in0=t_z[dc][:, 1:1 + SS],
                    scalar1=t_cw[:, 4 * dc:4 * dc + 1],
                    scalar2=t_cb[:, dc:dc + 1], op0=OP.mult, op1=OP.add),
                    reads=["zt", "cw", "cb"], writes=[("u", dc)])
                for k in range(1, 4):
                    em.op("vector", lambda e, dc=dc, k=k:
                          e.scalar_tensor_tensor(
                              out=t_u[dc][:, :],
                              in0=t_z[dc][:, 1 + k:1 + k + SS],
                              scalar=t_cw[:, 4 * dc + k:4 * dc + k + 1],
                              in1=t_u[dc][:, :], op0=OP.mult, op1=OP.add),
                          reads=["zt", "cw", ("u", dc)],
                          writes=[("u", dc)])
                em.op("scalar", lambda e, dc=dc: e.copy(
                    out=t_ub[dc][:, :], in_=t_u[dc][:, :]),
                    reads=[("u", dc)], writes=[("ub", dc)])
            em.mm([(lambda e, q=q, dc=dc: e.matmul(
                PSC[:, 512 * q:512 * (q + 1)],
                t_wxb1[:, RB * dc:RB * (dc + 1)],
                t_ub[dc][:, 512 * q:512 * (q + 1)],
                start=(dc == 0), stop=(dc == 1)))
                for q in range(4) for dc in range(2)],
                reads=[("ub", 0), ("ub", 1), ("wxb1", 0), ("wxb1", 1)],
                writes=["pscL", "pscH"])
            em.op("scalar", lambda e: e.copy(out=t_sp[:, :],
                                             in_=PSC[:, :]),
                  reads=["pscL", "pscH"], writes=["sp"])
            em.dma("gpsimd", ar_src[s][0:128, :], t_sp[:, :],
                   reads=["sp"], writes=[("arsrc", s)])
            em.mm([(lambda e, q=q, dc=dc: e.matmul(
                PSC[0:48, 512 * q:512 * (q + 1)],
                t_wxb2[:, 48 * dc:48 * (dc + 1)],
                t_ub[dc][:, 512 * q:512 * (q + 1)],
                start=(dc == 0), stop=(dc == 1)))
                for q in range(4) for dc in range(2)],
                reads=[("ub", 0), ("ub", 1), ("wxb2", 0), ("wxb2", 1)],
                writes=["pscL", "pscH"])
            em.op("scalar", lambda e: e.copy(out=t_sp[0:48, :],
                                             in_=PSC[0:48, :]),
                  reads=["pscL", "pscH"], writes=["sp"])
            em.dma("gpsimd", ar_src[s][128:144, :], t_sp[0:16, :],
                   reads=["sp"], writes=[("arsrc", s)])
            em.dma("gpsimd", ar_src[s][144:160, :], t_sp[32:48, :],
                   reads=["sp"], writes=[("arsrc", s)])
            em.coll("AllReduce", OP.add, ar_src[s][0:160, :],
                    ar_dst[s][0:160, :],
                    reads=[("arsrc", s)], writes=[("ar", s)])

        _nseg = int(os.environ.get("K_NSEG", str(NSEG)))
        for s in range(_nseg):
            scan_sublayer(0, s)

        def ffn_phase(g, first, x_read_fn, ydst, ydst_key, xstash,
                      xout_base, xout_cols, xout_key, post_stats):
            stash_key = ("stash", g, first)
            for dc in range(NDC):
                rk = x_read_fn(dc)
                em.dma("sync", t_y1r[:, :],
                       ydst[128 * dc:128 * (dc + 1), :],
                       reads=[ydst_key], writes=["tyr"])
                em.op("vector", lambda e: e.tensor_tensor(
                    out=t_x[:, :], in0=t_xa[:, :], in1=t_y1r[:, :],
                    op=OP.add), reads=rk + ["tyr"], writes=["tx"])
                em.dma("gpsimd", xstash[128 * dc:128 * (dc + 1), :],
                       t_x[:, :], reads=["tx"], writes=[stash_key])
                stat_mms(dc, ["tx"])
            ln_finalize("srow")
            em.mm([lambda e: e.matmul(PD[0][:, :], t_ones[0:1, :],
                                      t_srow[0:1, :],
                                      start=True, stop=True)],
                  reads=["srow", "ones"], writes=[("pd", 0)])
            em.mm([lambda e: e.matmul(PD[1][:, :], t_ones[32:33, :],
                                      t_srow[32:33, :],
                                      start=True, stop=True)],
                  reads=["srow", "ones"], writes=[("pd", 1)])
            for dc in range(NDC):
                em.dma("sync", t_xa[:, :],
                       xstash[128 * dc:128 * (dc + 1), :],
                       reads=[stash_key], writes=["txa"])
                em.op("vector", lambda e: e.tensor_tensor(
                    out=t_sg[:, :], in0=t_xa[:, :], in1=PD[0][:, :],
                    op=OP.add), reads=["txa", ("pd", 0)], writes=["tsg"])
                em.op("vector", lambda e, dc=dc: e.tensor_tensor(
                    out=t_z2[:, TBLK * dc:TBLK * (dc + 1)], in0=t_sg[:, :],
                    in1=PD[1][:, :], op=OP.mult),
                    reads=["tsg", ("pd", 1)], writes=[("z2", dc)])
            gp = g_stash[0] if first else g_stash[1]
            for hci in range(NHC):
                par = hci % 2
                em.dma("sync", t_wgu[par][:, 0:2048], wg_b[hci],
                       writes=[("wgu", par)])
                em.dma("sync", t_wgu[par][:, 2048:4096], wu_b[hci],
                       writes=[("wgu", par)])
                em.mm([(lambda e, dc=dc, par=par: e.matmul(
                    PG[:, :], t_wgu[par][:, 128 * dc:128 * (dc + 1)],
                    t_z2[:, TBLK * dc:TBLK * (dc + 1)],
                    start=(dc == 0), stop=(dc == NDC - 1)))
                    for dc in range(NDC)],
                    reads=[("wgu", par)] + [("z2", dc) for dc in range(NDC)],
                    writes=["pg"])
                em.mm([(lambda e, dc=dc, par=par: e.matmul(
                    PU[:, :],
                    t_wgu[par][:, 2048 + 128 * dc:2048 + 128 * (dc + 1)],
                    t_z2[:, TBLK * dc:TBLK * (dc + 1)],
                    start=(dc == 0), stop=(dc == NDC - 1)))
                    for dc in range(NDC)],
                    reads=[("wgu", par)] + [("z2", dc) for dc in range(NDC)],
                    writes=["pu"])
                em.op("scalar", lambda e: e.activation(
                    out=t_sg[:, :], in_=PG[:, :], func=AF.Silu),
                    reads=["pg"], writes=["tsg"])
                em.op("scalar", lambda e: e.copy(out=t_su[:, :], in_=PU[:, :]),
                      reads=["pu"], writes=["tsu"])
                em.op("gpsimd", lambda e: e.tensor_tensor(
                    out=t_g[:, :], in0=t_sg[:, :], in1=t_su[:, :],
                    op=OP.mult), reads=["tsg", "tsu"], writes=["tg"])
                em.dma("gpsimd", gp[:, TBLK * hci:TBLK * (hci + 1)],
                       t_g[:, :], reads=["tg"], writes=[("gst", hci)])
            for dc in range(NDC):
                par = dc % 2
                em.dma("sync", t_wd[par][:, :], wd_b[dc][:, 0:4096],
                       writes=[("wd", par)])
                pdp = dc % 2
                for q in range(16):
                    gpar = q % 2
                    em.dma("sync", t_gs[gpar][:, :],
                           gp[:, TBLK * 4 * q:TBLK * 4 * (q + 1)],
                           reads=[("gsdone", gpar)] +
                                 [("gst", 4 * q + i) for i in range(4)],
                           writes=[("gs", gpar)])
                    if q == 8:
                        em.dma("sync", t_wd[par][:, :], wd_b[dc][:, 4096:H],
                               reads=[("wdread", par)], writes=[("wd", par)])
                    em.mm([(lambda e, k=k, q=q, par=par, pdp=pdp, gpar=gpar:
                            e.matmul(
                                PD[pdp][:, :],
                                t_wd[par][:, 128 * ((4 * q + k) % 32):
                                          128 * ((4 * q + k) % 32 + 1)],
                                t_gs[gpar][:, TBLK * k:TBLK * (k + 1)],
                                start=(q == 0 and k == 0),
                                stop=(q == 15 and k == 3)))
                           for k in range(4)],
                          reads=[("wd", par), ("gs", gpar)],
                          writes=[("pd", pdp), ("gsdone", gpar),
                                  ("wdread", par)])
                em.dma("sync", t_xa[:, :],
                       xstash[128 * dc:128 * (dc + 1), :],
                       reads=[stash_key], writes=["txa"])
                em.op("vector", lambda e, pdp=pdp: e.scalar_tensor_tensor(
                    out=t_x[:, :], in0=PD[pdp][:, :], scalar=1.0,
                    in1=t_xa[:, :], op0=OP.mult, op1=OP.add),
                    reads=[("pd", pdp), "txa"], writes=["tx"])
                em.dma("gpsimd",
                       xout_base[128 * dc:128 * (dc + 1), xout_cols],
                       t_x[:, :], reads=["tx"], writes=[xout_key])
                if post_stats:
                    stat_mms(dc, ["tx"])
            if post_stats:
                ln_finalize("srow")
                em.dma("gpsimd", s3_src[g][0:1, :], t_srow[0:1, :],
                       reads=["srow"], writes=[("s3s", g)])
                em.dma("gpsimd", s3_src[g][1:2, :], t_srow[32:33, :],
                       reads=["srow"], writes=[("s3s", g)])
                em.coll("AllGather", OP.bypass, s3_src[g], s3_dst[g],
                        reads=[("s3s", g)], writes=[("s3d", g)])
                em.coll("AllToAll", OP.bypass, x2_src[g], x2_dst[g],
                        reads=[xout_key], writes=[("x2d", g)])

        if stage == 1:
            for g in range(NCH):
                tb = slice(TBLK * g, TBLK * (g + 1))
                have_y = ("y1d", g) in em.bw
                for dc in range(NDC):
                    em.dma("sync", t_xa[:, :],
                           x0_tok[128 * dc:128 * (dc + 1), tb],
                           writes=["txa"])
                    if have_y:
                        em.dma("sync", t_y1r[:, :],
                               y1_dst[g][128 * dc:128 * (dc + 1), :],
                               reads=[("y1d", g)], writes=["tyr"])
                        em.op("vector", lambda e: e.tensor_tensor(
                            out=t_x[:, :], in0=t_xa[:, :], in1=t_y1r[:, :],
                            op=OP.add), reads=["txa", "tyr"], writes=["tx"])
                    else:
                        em.op("vector", lambda e: e.tensor_copy(
                            out=t_x[:, :], in_=t_xa[:, :]),
                            reads=["txa"], writes=["tx"])
                    em.dma("sync", y_out[128 * dc:128 * (dc + 1), tb],
                           t_x[:, :], reads=["tx"], writes=["out"])
        else:
            for g in range(NCH):
                tb = slice(TBLK * g, TBLK * (g + 1))

                def read_x0(dc, tb=tb):
                    em.dma("sync", t_xa[:, :],
                           x0_tok[128 * dc:128 * (dc + 1), tb],
                           writes=["txa"])
                    return ["txa"]
                ffn_phase(g, True, read_x0, y1_dst[g], ("y1d", g),
                          x1_stash[g], x2_src[g], slice(0, TBLK),
                          ("xout", g, 1), post_stats=(stage >= 3))
                if stage == 2:
                    for dc in range(NDC):
                        em.dma("sync", t_x[:, :],
                               x2_src[g][128 * dc:128 * (dc + 1), :],
                               reads=[("xout", g, 1)], writes=["tx"])
                        em.dma("sync", y_out[128 * dc:128 * (dc + 1), tb],
                               t_x[:, :], reads=["tx"], writes=["out"])
            if stage >= 3:
                for s in range(NSEG):
                    scanB_part1(s)
                    scan_sublayer(1, s)
                for g in range(NCH):
                    tb = slice(TBLK * g, TBLK * (g + 1))
                    if stage == 3:
                        for dc in range(NDC):
                            em.dma("sync", t_xa[:, :],
                                   x2_src[g][128 * dc:128 * (dc + 1), :],
                                   reads=[("xout", g, 1)], writes=["txa"])
                            em.dma("sync", t_y1r[:, :],
                                   y3_dst[g][128 * dc:128 * (dc + 1), :],
                                   reads=[("y3d", g)], writes=["tyr"])
                            em.op("vector", lambda e: e.tensor_tensor(
                                out=t_x[:, :], in0=t_xa[:, :],
                                in1=t_y1r[:, :], op=OP.add),
                                reads=["txa", "tyr"], writes=["tx"])
                            em.dma("sync",
                                   y_out[128 * dc:128 * (dc + 1), tb],
                                   t_x[:, :], reads=["tx"], writes=["out"])
                    else:
                        def read_x2(dc, g=g):
                            em.dma("sync", t_xa[:, :],
                                   x2_src[g][128 * dc:128 * (dc + 1), :],
                                   reads=[("xout", g, 1)], writes=["txa"])
                            return ["txa"]
                        ffn_phase(g, False, read_x2, y3_dst[g], ("y3d", g),
                                  x3_stash[g], y_out, tb,
                                  ("out", g), post_stats=False)

        if "y_out" in em.dsems:
            osem, ocnt = em.dsems["y_out"][0], em.dsems["y_out"][1]
            em.streams["sync"].append(
                lambda e, s=osem, c=ocnt: e.wait_ge(s, c))

        with nc.Block() as block:
            em.finish(block)

    return nc


def _prep_inputs(inputs):
    x0 = np.asarray(inputs["L0_attn_a_x"])[0]
    x0t = np.ascontiguousarray(x0.T)
    wxa = np.asarray(inputs["L0_attn_a_W_xproj"])
    wdt = np.asarray(inputs["L0_attn_a_W_dt"])
    bdt = np.asarray(inputs["L0_attn_a_b_dt"])
    alog = np.asarray(inputs["L0_attn_a_A_log"])
    dparam = np.asarray(inputs["L0_attn_a_D_param"])
    cw = np.asarray(inputs["L0_attn_b_conv_w"])[:, 0, :]
    cb = np.asarray(inputs["L0_attn_b_conv_b"])
    wxb = np.asarray(inputs["L0_attn_b_xproj_w"])
    wdtb = np.asarray(inputs["L0_attn_b_dtproj_w"])
    bdtb = np.asarray(inputs["L0_attn_b_dtproj_b"])
    alogb = np.asarray(inputs["L0_attn_b_A_log"])
    dskip = np.asarray(inputs["L0_attn_b_D_skip"])
    wg = np.asarray(inputs["L0_ffn_w_gate"])
    wu = np.asarray(inputs["L0_ffn_w_up"])
    wd = np.asarray(inputs["L0_ffn_w_down"])
    f32 = np.float32

    def pack48(w, r0):
        out = np.zeros((D, 48), f32)
        out[:, 0:16] = w[r0:r0 + 16].T
        out[:, 32:48] = w[r0 + 16:r0 + 32].T
        return out

    wxa1_h = np.ascontiguousarray(wxa[0:64].T)
    wxa2_h = pack48(wxa, 64)
    wxb1_full = np.ascontiguousarray(wxb[0:128].T)
    wxb2_full = pack48(wxb, 128)

    sel_h = np.zeros((NST, NST * 128), f32)
    for n in range(NST):
        sel_h[n, 128 * n:128 * (n + 1)] = 1.0
    ones_h = np.ones((1, 128), f32)

    wg_h = np.ascontiguousarray(
        wg.T.reshape(NDC, 128, NHC, 128).transpose(2, 1, 0, 3)
        .reshape(NHC, 128, 2048))
    wu_h = np.ascontiguousarray(
        wu.T.reshape(NDC, 128, NHC, 128).transpose(2, 1, 0, 3)
        .reshape(NHC, 128, 2048))
    wd_h = np.ascontiguousarray(
        wd.T.reshape(NHC, 128, NDC, 128).transpose(2, 1, 0, 3)
        .reshape(NDC, 128, H))

    def dsh2(v):
        return np.ascontiguousarray(v.reshape(2, 128).T)

    def dsh2n(v):
        return np.ascontiguousarray(
            v.reshape(2, 128, NST).transpose(1, 0, 2).reshape(128, 2 * NST))

    in_maps = []
    for j in range(NC):
        ds = slice(DL * j, DL * (j + 1))
        blocks = [x0t[:, TBLK * (8 * c + j):TBLK * (8 * c + j + 1)]
                  for c in range(NCH)]
        m = {
            "x0_tok": np.ascontiguousarray(np.concatenate(blocks, axis=1)),
            "x0_dsh": np.ascontiguousarray(x0t[ds, :]),
            "wxa1": wxa1_h, "wxa2": wxa2_h,
            "wxb1": np.ascontiguousarray(wxb1_full[ds, :]),
            "wxb2": np.ascontiguousarray(wxb2_full[ds, :]),
            "wdta": np.ascontiguousarray(wdt[ds, :].T),
            "wdtb": np.ascontiguousarray(wdtb[ds, :].T),
            "bdta": dsh2(bdt[ds]),
            "bdtb": dsh2(bdtb[ds]),
            "aloga": dsh2n(alog[ds, :]),
            "alogb": dsh2n(alogb[ds, :]),
            "dpar": dsh2(dparam[ds]),
            "dskp": dsh2(dskip[ds]),
            "cwin": np.ascontiguousarray(
                cw[ds, :].reshape(2, 128, 4).transpose(1, 0, 2)
                .reshape(128, 8)),
            "cbin": dsh2(cb[ds]),
            "selin": sel_h, "onesin": ones_h,
            "wg_s": wg_h, "wu_s": wu_h, "wd_s": wd_h,
        }
        in_maps.append(m)
    return in_maps


def _assemble(results):
    out = np.empty((L, D), np.float32)
    for j in range(NC):
        y = results[j]["y_out"]
        for c in range(NCH):
            b = 8 * c + j
            out[TBLK * b:TBLK * (b + 1), :] = y[:, TBLK * c:TBLK * (c + 1)].T
    return out.reshape(1, L, D)


def run(inputs, stage=4):
    in_maps = _prep_inputs(inputs)
    nc = build_nc(stage)
    res = run_bass_kernel_spmd(nc, in_maps, core_ids=list(range(NC)))
    return _assemble(res.results)


def kernel(**inputs):
    return run(inputs, stage=4)


# version 11
# speedup vs baseline: 1.3335x; 1.3335x over previous
import contextlib
import os
import re
import numpy as np
import concourse.bass as bass
import concourse.mybir as mybir
from concourse.bass_utils import run_bass_kernel_spmd
from concourse.aot_env import allow_ici_all_to_all

F32 = mybir.dt.float32
BF16 = mybir.dt.bfloat16
AF = mybir.ActivationFunctionType
OP = mybir.AluOpType

NC = 8
L = 16384
D = 2048
H = 8192
NST = 16
RA = 64
RB = 128
TBLK = 512
SS = 2048
NSEG = L // SS
NCH = 4
DL = D // NC
NDC = 16
LT = L // NC
EPS = 1e-5
NHC = H // 128


class Em:

    ENGINES = ("tensor", "vector", "scalar", "gpsimd", "sync")

    def __init__(self, nc, ctx):
        self.nc = nc
        self.streams = {e: [] for e in self.ENGINES}
        self.sems = {e: ctx.enter_context(nc.semaphore(f"m_{e}"))
                     for e in self.ENGINES}
        self.counts = {e: 0 for e in self.ENGINES}
        self.ctx = ctx
        self.dsems = {}
        self.ccsems = []
        self.bw = {}
        self.br = {}

    def _sem_of(self, tag):
        if tag in self.sems:
            return self.sems[tag]
        if isinstance(tag, tuple) and tag[0] == "dk":
            return self.dsems[tag[1]][0]
        if isinstance(tag, tuple) and tag[0] == "cc":
            return self.ccsems[tag[1]]
        raise KeyError(tag)

    def _emit_waits(self, stream_eng, waits):
        noself = os.environ.get("K_SELF") != "1"
        best = {}
        for tag, count in waits:
            if count <= 0 or (noself and tag == stream_eng):
                continue
            if best.get(tag, 0) < count:
                best[tag] = count
        for tag, count in best.items():
            if (isinstance(tag, tuple) and tag[0] == "dk"
                    and tag[1] in self.FINAL_POOLS):
                ent = self.dsems[tag[1]]
                self.streams[stream_eng].append(
                    lambda e, ent=ent: e.wait_ge(ent[0], ent[1]))
            else:
                sem = self._sem_of(tag)
                self.streams[stream_eng].append(
                    lambda e, s=sem, c=count: e.wait_ge(s, c))

    @staticmethod
    def _is_psum_key(k):
        return (k in ("pscL", "pscH", "pg", "pu")
                or (isinstance(k, tuple) and k and k[0] == "pd"))

    def _dep_waits(self, reads, writes):
        waits = []
        for k in reads:
            if k in self.bw:
                waits.append(self.bw[k])
            if self._is_psum_key(k):
                waits.extend(self.br.get(k, {}).items())
        for k in writes:
            if k in self.bw:
                waits.append(self.bw[k])
            waits.extend(self.br.get(k, {}).items())
        return waits

    def _record(self, tag, count, reads, writes):
        for k in writes:
            self.bw[k] = (tag, count)
            self.br[k] = {}
        for k in reads:
            self.br.setdefault(k, {})[tag] = count

    def op(self, eng, fn, reads=(), writes=()):
        self._emit_waits(eng, self._dep_waits(reads, writes))
        self.counts[eng] += 1
        cnt, sem = self.counts[eng], self.sems[eng]

        def run(e, fn=fn, sem=sem):
            fn(e).then_inc(sem, 1)
        self.streams[eng].append(run)
        self._record(eng, cnt, reads, writes)
        return (eng, cnt)

    def mm(self, fns, reads=(), writes=()):
        self._emit_waits("tensor", self._dep_waits(reads, writes))
        self.counts["tensor"] += 1
        cnt, sem = self.counts["tensor"], self.sems["tensor"]

        def run(e, fns=fns, sem=sem):
            last = None
            for f in fns:
                last = f(e)
            last.then_inc(sem, 1)
        self.streams["tensor"].append(run)
        self._record("tensor", cnt, reads, writes)
        return ("tensor", cnt)

    FINAL_POOLS = ("init",)

    def dma(self, queue, out, in_, reads=(), writes=(), pool=None,
            sname=None):
        if pool is None:
            self._emit_waits(queue, self._dep_waits(reads, writes))
        else:
            self._emit_waits(queue, [w for w in
                                     self._dep_waits(reads, ())])
        name = pool or sname or out.tensor.name
        if name not in self.dsems:
            self.dsems[name] = [
                self.ctx.enter_context(self.nc.semaphore(f"dk_{name}")), 0]
        ent = self.dsems[name]
        ent[1] += 16
        cnt, sem = ent[1], ent[0]
        if cnt > 16:
            self.streams[queue].append(
                lambda e, s=sem, c=cnt - 16: e.wait_ge(s, c))

        def run(e, out=out, in_=in_, sem=sem):
            e.dma_start(out=out, in_=in_).then_inc(sem, 16)
        self.streams[queue].append(run)
        self._record(("dk", name), cnt, reads, writes)
        return (("dk", name), cnt)

    def coll(self, kind, op, in_ap, out_ap, reads=(), writes=()):
        self._emit_waits("gpsimd", self._dep_waits(reads, writes))
        idx = len(self.ccsems)
        sem = self.ctx.enter_context(self.nc.semaphore(f"cc_{idx}"))
        self.ccsems.append(sem)

        def run(e, kind=kind, op=op, i=in_ap, o=out_ap, sem=sem):
            with allow_ici_all_to_all():
                e.collective_compute(
                    kind, op, replica_groups=[list(range(NC))],
                    ins=[i], outs=[o]).then_inc(sem)
        self.streams["gpsimd"].append(run)
        self._record(("cc", idx), 1, reads, writes)
        return (("cc", idx), 1)

    def finish(self, block):
        for eng in self.ENGINES:
            stream = self.streams[eng]

            def body(e, stream=stream):
                for f in stream:
                    f(e)
            getattr(block, eng)(body)


def build_nc(stage=4):
    nc = bass.Bass("TRN2", target_bir_lowering=False)

    def ein(name, shape, dt=F32):
        return nc.dram_tensor(name, shape, dt, kind="ExternalInput").ap()

    def idr(name, shape, dt=F32):
        return nc.dram_tensor(name, shape, dt).ap()

    x0_tok = ein("x0_tok", [D, LT])
    x0_dsh = ein("x0_dsh", [DL, L])
    wxa1 = ein("wxa1", [D, RA])
    wxa2 = ein("wxa2", [D, 48])
    wxb1 = ein("wxb1", [DL, RB])
    wxb2 = ein("wxb2", [DL, 48])
    wdta = ein("wdta", [RA, DL])
    wdtb = ein("wdtb", [RB, DL])
    bdta = ein("bdta", [128, 2])
    bdtb = ein("bdtb", [128, 2])
    aloga = ein("aloga", [128, 2 * NST])
    alogb = ein("alogb", [128, 2 * NST])
    dpar = ein("dpar", [128, 2])
    dskp = ein("dskp", [128, 2])
    cwin = ein("cwin", [128, 8])
    cbin = ein("cbin", [128, 2])
    selin = ein("selin", [NST, NST * 128])
    onesin = ein("onesin", [1, 128])
    wg_s = ein("wg_s", [NHC, 128, 2048])
    wu_s = ein("wu_s", [NHC, 128, 2048])
    wd_s = ein("wd_s", [NHC, 128, 2048])

    y_out = nc.dram_tensor("y_out", [D, LT], F32, kind="ExternalOutput").ap()

    wg_b = idr("wg_b", [NHC, 128, 2048], BF16)
    wu_b = idr("wu_b", [NHC, 128, 2048], BF16)
    wd_b = idr("wd_b", [NHC, 128, 2048], BF16)
    gA_src = [idr(f"gA_src{g}", [98, TBLK]) for g in range(NCH)]
    gA_dst = [idr(f"gA_dst{g}", [NC * 98, TBLK]) for g in range(NCH)]
    y1_src = [idr(f"y1_src{g}", [D, TBLK]) for g in range(NCH)]
    y1_dst = [idr(f"y1_dst{g}", [D, TBLK]) for g in range(NCH)]
    x2_src = [idr(f"x2_src{g}", [D, TBLK]) for g in range(NCH)]
    x2_dst = [idr(f"x2_dst{g}", [D, TBLK]) for g in range(NCH)]
    y3_src = [idr(f"y3_src{g}", [D, TBLK]) for g in range(NCH)]
    y3_dst = [idr(f"y3_dst{g}", [D, TBLK]) for g in range(NCH)]
    s3_src = [idr(f"s3_src{g}", [2, TBLK]) for g in range(NCH)]
    s3_dst = [idr(f"s3_dst{g}", [NC * 2, TBLK]) for g in range(NCH)]
    ar_src = [idr(f"ar_src{s}", [176, SS]) for s in range(NSEG)]
    ar_dst = [idr(f"ar_dst{s}", [176, SS]) for s in range(NSEG)]
    x1_stash = [idr(f"x1_stash{g}", [D, TBLK]) for g in range(NCH)]
    x3_stash = [idr(f"x3_stash{g}", [D, TBLK]) for g in range(NCH)]
    g_stash = [idr(f"g_stash{p}", [128, NHC * TBLK], BF16) for p in range(2)]

    with contextlib.ExitStack() as ctx:
        def sb(n, s, d=F32):
            return ctx.enter_context(nc.sbuf_tensor(n, s, d))

        em = Em(nc, ctx)

        t_wxa1 = sb("t_wxa1", [128, NDC * RA], BF16)
        t_wxa2 = sb("t_wxa2", [128, NDC * 48], BF16)
        t_wxb1 = sb("t_wxb1", [128, 2 * RB], BF16)
        t_wxb2 = sb("t_wxb2", [128, 2 * 48], BF16)
        t_wdta = sb("t_wdta", [RA, DL], BF16)
        t_wdtb = sb("t_wdtb", [RB, DL], BF16)
        t_sel = sb("t_sel", [NST, NST * 128], BF16)
        t_ones = sb("t_ones", [33, 128])
        t_onesb = sb("t_onesb", [1, 128], BF16)
        t_onescol = sb("t_onescol", [128, 1], BF16)
        t_bdta = sb("t_bdta", [128, 2])
        t_bdtb = sb("t_bdtb", [128, 2])
        t_anegA = sb("t_anegA", [128, 2 * NST])
        t_anegB = sb("t_anegB", [128, 2 * NST])
        t_dpar = sb("t_dpar", [128, 2])
        t_dskp = sb("t_dskp", [128, 2])
        t_cw = sb("t_cw", [128, 8])
        t_cb = sb("t_cb", [128, 2])
        t_wsumA = sb("t_wsumA", [112, 1])
        t_carA = [sb(f"t_carA{c}", [128, NST]) for c in range(2)]
        t_carB = [sb(f"t_carB{c}", [128, NST]) for c in range(2)]
        t_tail = [sb(f"t_tail{c}", [128, 4]) for c in range(2)]
        t_st1 = sb("t_st1", [128, SS])
        t_st3 = t_st1
        t_srow = sb("t_srow", [33, TBLK])
        t_svar = sb("t_svar", [33, TBLK])

        t_zz = sb("t_zz", [128, SS + 4])
        t_z = [t_zz, t_zz]
        t_u = [sb(f"t_u{c}", [128, SS]) for c in range(2)]
        t_ub = [sb(f"t_ub{c}", [128, SS], BF16) for c in range(2)]
        t_dt = [sb(f"t_dt{c}", [128, SS]) for c in range(2)]
        t_dx = [sb(f"t_dx{c}", [128, SS]) for c in range(2)]
        t_y = [sb(f"t_y{c}", [128, SS]) for c in range(2)]
        t_sp = sb("t_sp", [128, SS])
        t_b = [sb(f"t_b{c}", [128, SS]) for c in range(2)]
        t_h = sb("t_h", [128, SS])
        t_cbc = t_st1
        t_dtrBb = t_ub[0][0:RB, :]
        t_dtrAb = t_ub[0][0:RA, :]
        t_bbb = sb("t_bbb", [NST, SS], BF16)
        t_crb = sb("t_crb", [NST, SS], BF16)

        t_x = [sb(f"t_x_{p}", [128, TBLK]) for p in range(2)]
        t_xa = [sb(f"t_xa_{p}", [128, TBLK]) for p in range(2)]
        t_y1r = [sb(f"t_y1r_{p}", [128, TBLK]) for p in range(2)]
        t_xb16_s = sb("t_xb16", [128, 2 * TBLK], BF16)
        t_xb16 = [t_xb16_s, t_xb16_s]
        t_z2 = sb("t_z2", [128, NDC * TBLK], BF16)
        t_sg = [sb(f"t_sg_{p}", [128, TBLK]) for p in range(2)]
        t_su = [sb(f"t_su_{p}", [128, TBLK]) for p in range(2)]
        t_g = [sb(f"t_g_{p}", [128, TBLK], BF16) for p in range(2)]
        t_gs = [sb(f"t_gs{p}", [128, 4 * TBLK], BF16) for p in range(2)]
        t_wgu = [sb(f"t_wgu{p}", [128, 4096], BF16) for p in range(2)]
        t_wd = t_wgu
        t_xdb = t_sp[0:112, 0:TBLK]

        PSC = ctx.enter_context(nc.psum_tensor("PSC", [128, 2048], F32))
        PF = ctx.enter_context(nc.psum_tensor("PF", [128, 2048], F32))
        PG = PF[:, 0:512]
        PU = PF[:, 512:1024]
        PD = [PF[:, 1024:1536], PF[:, 1536:2048]]


        em.dma("sync", t_ones[0:1, :], onesin, pool="init", writes=["ones"])
        em.dma("sync", t_ones[32:33, :], onesin, pool="init", writes=["ones"])
        em.op("scalar", lambda e: e.copy(out=t_onesb[:, :], in_=t_ones[0:1, :]),
              reads=["ones"], writes=["onesb"])
        em.op("vector", lambda e: e.memset(t_onescol[:, :], 1.0),
              writes=["onescol"])
        for c in range(2):
            em.op("vector", lambda e, c=c: e.memset(t_carA[c][:, :], 0.0),
                  writes=[("carA", c)])
            em.op("vector", lambda e, c=c: e.memset(t_carB[c][:, :], 0.0),
                  writes=[("carB", c)])
            em.op("vector", lambda e, c=c: e.memset(t_tail[c][:, :], 0.0),
                  writes=[("tail", c)])

        em.dma("sync", t_bdta[:, :], bdta, pool="init", writes=["bdta"])
        em.dma("sync", t_bdtb[:, :], bdtb, pool="init", writes=["bdtb"])
        em.dma("sync", t_dpar[:, :], dpar, pool="init", writes=["dpar"])
        em.dma("sync", t_dskp[:, :], dskp, pool="init", writes=["dskp"])
        em.dma("sync", t_cw[:, :], cwin, pool="init", writes=["cw"])
        em.dma("sync", t_cb[:, :], cbin, pool="init", writes=["cb"])
        em.dma("sync", t_anegA[:, :], aloga, pool="init", writes=["anegA"])
        em.dma("sync", t_anegB[:, :], alogb, pool="init", writes=["anegB"])

        def load_cast(key, dst, src, rows, cols):
            em.dma("sync", t_sp[0:rows, 0:cols], src, writes=["sp"])
            em.op("scalar", lambda e, r=rows, co=cols, d=dst: e.copy(
                out=d, in_=t_sp[0:r, 0:co]),
                reads=["sp"], writes=[key])

        for dc in range(NDC):
            load_cast(("wxa1", dc), t_wxa1[:, RA * dc:RA * (dc + 1)],
                      wxa1[128 * dc:128 * (dc + 1), :], 128, RA)
            load_cast(("wxa2", dc), t_wxa2[:, 48 * dc:48 * (dc + 1)],
                      wxa2[128 * dc:128 * (dc + 1), :], 128, 48)
        for c in range(2):
            load_cast(("wxb1", c), t_wxb1[:, RB * c:RB * (c + 1)],
                      wxb1[128 * c:128 * (c + 1), :], 128, RB)
            load_cast(("wxb2", c), t_wxb2[:, 48 * c:48 * (c + 1)],
                      wxb2[128 * c:128 * (c + 1), :], 128, 48)
        load_cast("wdta", t_wdta[:, :], wdta, RA, DL)
        load_cast("wdtb", t_wdtb[:, :], wdtb, RB, DL)
        load_cast("sel", t_sel[:, :], selin, NST, NST * 128)
        for t_an, key in ((t_anegA, "anegA"), (t_anegB, "anegB")):
            em.op("scalar", lambda e, t=t_an: e.activation(
                out=t[:, :], in_=t[:, :], func=AF.Exp),
                reads=[key], writes=[key])
            em.op("vector", lambda e, t=t_an: e.tensor_scalar(
                out=t[:, :], in0=t[:, :], scalar1=-1.0, scalar2=None,
                op0=OP.mult), reads=[key], writes=[key])

        em.mm([(lambda e, dc=dc: e.matmul(
            PSC[0:RA, 0:1], t_wxa1[:, RA * dc:RA * (dc + 1)],
            t_onescol[:, :], start=(dc == 0), stop=(dc == NDC - 1)))
            for dc in range(NDC)],
            reads=[("wxa1", dc) for dc in range(NDC)] + ["onescol"],
            writes=["pscL"])
        em.op("scalar", lambda e: e.copy(out=t_wsumA[0:RA, :],
                                         in_=PSC[0:RA, 0:1]),
              reads=["pscL"], writes=["wsumA"])
        em.mm([(lambda e, dc=dc: e.matmul(
            PSC[0:48, 0:1], t_wxa2[:, 48 * dc:48 * (dc + 1)],
            t_onescol[:, :], start=(dc == 0), stop=(dc == NDC - 1)))
            for dc in range(NDC)],
            reads=[("wxa2", dc) for dc in range(NDC)] + ["onescol"],
            writes=["pscL"])
        em.op("scalar", lambda e: e.copy(out=t_wsumA[64:112, :],
                                         in_=PSC[0:48, 0:1]),
              reads=["pscL"], writes=["wsumA"])

        if stage >= 2:
            def convert(srcT, dstT, nslab, kind):
                for i in range(nslab):
                    em.dma("gpsimd", dstT[i], srcT[i],
                           writes=[("wb", kind, i % 4)],
                           sname=f"wcv{i % 4}")
            convert(wg_s, wg_b, NHC, "g")
            convert(wu_s, wu_b, NHC, "u")
            convert(wd_s, wd_b, NHC, "d")

        def ln_finalize(srow_key):
            em.op("vector", lambda e: e.tensor_scalar(
                out=t_srow[0:1, :], in0=PG[0:1, :], scalar1=-1.0 / D,
                scalar2=None, op0=OP.mult), reads=["pg"], writes=[srow_key])
            em.op("scalar", lambda e: e.activation(
                out=t_svar[0:1, :], in_=t_srow[0:1, :],
                func=AF.Square), reads=[srow_key], writes=["svar"])
            em.op("vector", lambda e: e.scalar_tensor_tensor(
                out=t_svar[32:33, :], in0=PU[0:1, :],
                scalar=1.0 / D, in1=t_svar[0:1, :],
                op0=OP.mult, op1=OP.subtract),
                reads=["pu", "svar"], writes=["svar"])
            em.op("vector", lambda e: e.tensor_scalar(
                out=t_svar[32:33, :], in0=t_svar[32:33, :], scalar1=EPS,
                scalar2=None, op0=OP.add), reads=["svar"], writes=["svar"])
            em.op("scalar", lambda e: e.activation(
                out=t_svar[32:33, :], in_=t_svar[32:33, :],
                func=AF.Ln), reads=["svar"], writes=["svar"])
            em.op("scalar", lambda e: e.activation(
                out=t_srow[32:33, :], in_=t_svar[32:33, :],
                func=AF.Exp, scale=-0.5), reads=["svar"], writes=[srow_key])

        def stat_mms(dc, xkeys):
            par = dc % 2
            xb = t_xb16[par]
            em.op("scalar", lambda e, par=par, xb=xb: e.copy(
                out=xb[:, 0:TBLK], in_=t_x[par][:, :]),
                reads=xkeys, writes=["xb16"])
            em.op("scalar", lambda e, par=par, xb=xb: e.activation(
                out=xb[:, TBLK:2 * TBLK], in_=t_x[par][:, :],
                func=AF.Square), reads=xkeys, writes=["xb16"])
            em.mm([lambda e, dc=dc, xb=xb: e.matmul(
                PG[0:1, :], t_onescol[:, :], xb[:, 0:TBLK],
                start=(dc == 0), stop=(dc == NDC - 1))],
                reads=["xb16", "onescol"], writes=["pg"])
            em.mm([lambda e, dc=dc, xb=xb: e.matmul(
                PU[0:1, :], t_onescol[:, :], xb[:, TBLK:2 * TBLK],
                start=(dc == 0), stop=(dc == NDC - 1))],
                reads=["xb16", "onescol"], writes=["pu"])

        _pre = int(os.environ.get("K_PRE", str(NCH)))
        for g in range(_pre):
            tb = slice(TBLK * g, TBLK * (g + 1))
            for dc in range(NDC):
                par = dc % 2
                em.dma("sync", t_x[par][:, :],
                       x0_tok[128 * dc:128 * (dc + 1), tb],
                       writes=[("tx", par)])
                stat_mms(dc, [("tx", par)])
                em.mm([lambda e, dc=dc, par=par: e.matmul(
                    PSC[0:RA, 0:TBLK], t_wxa1[:, RA * dc:RA * (dc + 1)],
                    t_xb16[par][:, 0:TBLK],
                    start=(dc == 0), stop=(dc == NDC - 1))],
                    reads=["xb16", ("wxa1", dc)], writes=["pscL"])
                em.mm([lambda e, dc=dc, par=par: e.matmul(
                    PSC[0:48, TBLK:2 * TBLK], t_wxa2[:, 48 * dc:48 * (dc + 1)],
                    t_xb16[par][:, 0:TBLK],
                    start=(dc == 0), stop=(dc == NDC - 1))],
                    reads=["xb16", ("wxa2", dc)], writes=["pscL"])
            ln_finalize("srow")
            em.mm([lambda e: e.matmul(PD[0][:, :], t_ones[0:1, :],
                                      t_srow[0:1, :],
                                      start=True, stop=True)],
                  reads=["srow", "ones"], writes=[("pd", 0)])
            em.mm([lambda e: e.matmul(PD[1][:, :], t_ones[32:33, :],
                                      t_srow[32:33, :],
                                      start=True, stop=True)],
                  reads=["srow", "ones"], writes=[("pd", 1)])
            em.op("scalar", lambda e: e.copy(out=t_xdb[0:RA, :],
                                             in_=PSC[0:RA, 0:TBLK]),
                  reads=["pscL"], writes=["sp"])
            em.op("scalar", lambda e: e.copy(out=t_xdb[64:112, :],
                                             in_=PSC[0:48, TBLK:2 * TBLK]),
                  reads=["pscL"], writes=["sp"])
            em.op("vector", lambda e: e.scalar_tensor_tensor(
                out=t_xdb[0:112, :], in0=PD[0][0:112, :],
                scalar=t_wsumA[0:112, :], in1=t_xdb[0:112, :],
                op0=OP.mult, op1=OP.add),
                reads=[("pd", 0), "wsumA", "sp"], writes=["sp"])
            em.op("vector", lambda e: e.tensor_tensor(
                out=t_xdb[0:112, :], in0=t_xdb[0:112, :],
                in1=PD[1][0:112, :], op=OP.mult),
                reads=["sp", ("pd", 1)], writes=["sp"])
            em.dma("gpsimd", gA_src[g][0:64, :], t_xdb[0:64, :],
                   reads=["sp"], writes=[("gsrc", g)])
            em.dma("gpsimd", gA_src[g][64:80, :], t_xdb[64:80, :],
                   reads=["sp"], writes=[("gsrc", g)])
            em.dma("gpsimd", gA_src[g][80:96, :], t_xdb[96:112, :],
                   reads=["sp"], writes=[("gsrc", g)])
            em.dma("gpsimd", gA_src[g][96:97, :], t_srow[0:1, :],
                   reads=["srow"], writes=[("gsrc", g)])
            em.dma("gpsimd", gA_src[g][97:98, :], t_srow[32:33, :],
                   reads=["srow"], writes=[("gsrc", g)])
            em.coll("AllGather", OP.bypass, gA_src[g], gA_dst[g],
                    reads=[("gsrc", g)], writes=[("gA", g)])

        def ln_apply_seg(dc, dst_ap, src_ap, src_key, st_tile, st_key,
                         dst_key):
            for h2 in range(2):
                cs = slice(1024 * h2, 1024 * (h2 + 1))
                em.mm([(lambda e, q=q, h2=h2: e.matmul(
                    PSC[:, 512 * q:512 * (q + 1)], t_ones[0:1, :],
                    st_tile[0:1,
                            1024 * h2 + 512 * q:1024 * h2 + 512 * (q + 1)],
                    start=True, stop=True)) for q in range(2)],
                    reads=[st_key, "ones"], writes=["pscL"])
                em.mm([(lambda e, q=q, h2=h2: e.matmul(
                    PSC[:, 1024 + 512 * q:1024 + 512 * (q + 1)],
                    t_ones[32:33, :],
                    st_tile[32:33,
                            1024 * h2 + 512 * q:1024 * h2 + 512 * (q + 1)],
                    start=True, stop=True)) for q in range(2)],
                    reads=[st_key, "ones"], writes=["pscH"])
                em.op("vector", lambda e, cs=cs: e.tensor_tensor(
                    out=t_sp[:, cs], in0=src_ap[:, cs],
                    in1=PSC[:, 0:1024], op=OP.add),
                    reads=[src_key, "pscL"], writes=["sp"])
                em.op("vector", lambda e, cs=cs: e.tensor_tensor(
                    out=dst_ap[:, cs], in0=t_sp[:, cs],
                    in1=PSC[:, 1024:2048], op=OP.mult),
                    reads=["sp", "pscH"], writes=[dst_key])

        def softplus(dc, tbias, bias_key, dt_key):
            em.op("vector", lambda e, dc=dc: e.tensor_scalar(
                out=t_dt[dc][:, :], in0=PSC[:, :],
                scalar1=tbias[:, dc:dc + 1], scalar2=0.0,
                op0=OP.add, op1=OP.max),
                reads=["pscL", "pscH", bias_key], writes=[dt_key])
            em.op("scalar", lambda e, dc=dc: e.activation(
                out=t_sp[:, :], in_=PSC[:, :], func=AF.Abs,
                bias=tbias[:, dc:dc + 1]),
                reads=["pscL", "pscH", bias_key], writes=["sp"])
            em.op("scalar", lambda e: e.activation(
                out=t_sp[:, :], in_=t_sp[:, :], func=AF.Exp,
                scale=-1.0), reads=["sp"], writes=["sp"])
            em.op("scalar", lambda e: e.activation(
                out=t_sp[:, :], in_=t_sp[:, :], func=AF.Ln,
                bias=1.0), reads=["sp"], writes=["sp"])
            em.op("vector", lambda e, dc=dc: e.tensor_tensor(
                out=t_dt[dc][:, :], in0=t_dt[dc][:, :], in1=t_sp[:, :],
                op=OP.add), reads=[dt_key, "sp"], writes=[dt_key])

        def scan_sublayer(scan_id, s):
            _sub = int(os.environ.get("K_SCANSUB", "5"))
            g, half = s // 2, s % 2
            if scan_id == 0:
                t_dtrb = t_dtrAb
                wdt_t, wdt_key = t_wdta, "wdta"
                tbias, bias_key = t_bdta, "bdta"
                t_aneg, aneg_key = t_anegA, "anegA"
                t_car, car_key = t_carA, "carA"
                t_skip, skip_key = t_dpar, "dpar"
            else:
                t_dtrb = t_dtrBb
                wdt_t, wdt_key = t_wdtb, "wdtb"
                tbias, bias_key = t_bdtb, "bdtb"
                t_aneg, aneg_key = t_anegB, "anegB"
                t_car, car_key = t_carB, "carB"
                t_skip, skip_key = t_dskp, "dskp"

            if scan_id == 0:
                for k in range(4):
                    i = 4 * half + k
                    base = 98 * i
                    cseg = slice(512 * k, 512 * (k + 1))
                    em.dma("gpsimd", t_dtrb[:, cseg],
                           gA_dst[g][base:base + 64, :],
                           reads=[("gA", g)], writes=[("ub", 0)])
                    em.dma("gpsimd", t_bbb[:, cseg],
                           gA_dst[g][base + 64:base + 80, :],
                           reads=[("gA", g)], writes=["bbb"])
                    em.dma("gpsimd", t_crb[:, cseg],
                           gA_dst[g][base + 80:base + 96, :],
                           reads=[("gA", g)], writes=["crb"])
                    em.dma("sync", t_st1[0:1, cseg],
                           gA_dst[g][base + 96:base + 97, :],
                           reads=[("gA", g)], writes=["st1"])
                    em.dma("sync",
                           t_st1[32:33, 512 * k:512 * (k + 1)],
                           gA_dst[g][base + 97:base + 98, :],
                           reads=[("gA", g)], writes=["st1"])
            else:
                em.dma("gpsimd", t_dtrb[:, :], ar_dst[s][0:RB, :],
                       reads=[("ar", s)], writes=[("ub", 0)])
                em.dma("gpsimd", t_bbb[:, :], ar_dst[s][128:144, :],
                       reads=[("ar", s)], writes=["bbb"])
                em.dma("gpsimd", t_crb[:, :], ar_dst[s][144:160, :],
                       reads=[("ar", s)], writes=["crb"])

            if _sub < 2:
                return
            if scan_id == 0:
                for dc in range(2):
                    em.dma("sync", t_u[dc][:, :],
                           x0_dsh[128 * dc:128 * (dc + 1),
                                  SS * s:SS * (s + 1)],
                           writes=[("u", dc)])
                    ln_apply_seg(dc, t_u[dc], t_u[dc], ("u", dc),
                                 t_st1, "st1", ("u", dc))

            if _sub < 3:
                return
            _dtp = int(os.environ.get("K_DTPART", "3"))
            for dc in range(2):
                em.mm([(lambda e, q=q, dc=dc: e.matmul(
                    PSC[:, 512 * q:512 * (q + 1)],
                    wdt_t[:, 128 * dc:128 * (dc + 1)],
                    t_dtrb[:, 512 * q:512 * (q + 1)],
                    start=True, stop=True)) for q in range(4)],
                    reads=[("ub", 0), wdt_key], writes=["pscL", "pscH"])
                if _dtp == 4:
                    em.op("vector", lambda e, dc=dc: e.tensor_scalar(
                        out=t_dt[dc][:, :], in0=PSC[:, :],
                        scalar1=tbias[:, dc:dc + 1], scalar2=0.0,
                        op0=OP.add, op1=OP.max),
                        reads=["pscL", "pscH", bias_key], writes=[("dt", dc)])
                elif _dtp == 5:
                    em.op("scalar", lambda e, dc=dc: e.activation(
                        out=t_dt[dc][:, :], in_=PSC[:, :], func=AF.Abs,
                        bias=tbias[:, dc:dc + 1]),
                        reads=["pscL", "pscH", bias_key], writes=[("dt", dc)])
                elif _dtp >= 2:
                    softplus(dc, tbias, bias_key, ("dt", dc))
                else:
                    em.op("vector", lambda e, dc=dc: e.tensor_copy(
                        out=t_dt[dc][:, :], in_=PSC[:, :]),
                        reads=["pscL", "pscH"], writes=[("dt", dc)])
                if _dtp >= 3:
                    em.op("vector", lambda e, dc=dc: e.tensor_tensor(
                        out=t_dx[dc][:, :], in0=t_dt[dc][:, :],
                        in1=t_u[dc][:, :], op=OP.mult),
                        reads=[("dt", dc), ("u", dc)], writes=[("dx", dc)])
                    em.op("vector", lambda e, dc=dc: e.tensor_scalar(
                        out=t_y[dc][:, :], in0=t_u[dc][:, :],
                        scalar1=t_skip[:, dc:dc + 1], scalar2=None,
                        op0=OP.mult),
                        reads=[("u", dc), skip_key], writes=[("y", dc)])

            if _sub < 4:
                return
            for n in range(NST):
                for h2 in range(2):
                    cs = slice(1024 * h2, 1024 * (h2 + 1))
                    em.mm([(lambda e, q=q, n=n, h2=h2: e.matmul(
                        PSC[:, 512 * q:512 * (q + 1)],
                        t_sel[:, 128 * n:128 * (n + 1)],
                        t_bbb[:, 1024 * h2 + 512 * q:
                              1024 * h2 + 512 * (q + 1)],
                        start=True, stop=True)) for q in range(2)],
                        reads=["bbb", "sel"], writes=["pscL"])
                    em.mm([(lambda e, q=q, n=n, h2=h2: e.matmul(
                        PSC[:, 1024 + 512 * q:1024 + 512 * (q + 1)],
                        t_sel[:, 128 * n:128 * (n + 1)],
                        t_crb[:, 1024 * h2 + 512 * q:
                              1024 * h2 + 512 * (q + 1)],
                        start=True, stop=True)) for q in range(2)],
                        reads=["crb", "sel"], writes=["pscH"])
                    _ct = t_cbc if n % 2 == 0 else t_zz[:, 0:SS]
                    _ck = "st1" if n % 2 == 0 else "zt"
                    em.op("scalar", lambda e, cs=cs, _ct=_ct: e.copy(
                        out=_ct[:, cs], in_=PSC[:, 1024:2048]),
                        reads=["pscH"], writes=[_ck])
                    for dc in range(2):
                        em.op("vector", lambda e, dc=dc, cs=cs:
                              e.tensor_tensor(
                                  out=t_b[dc][:, cs], in0=t_dx[dc][:, cs],
                                  in1=PSC[:, 0:1024], op=OP.mult),
                              reads=[("dx", dc), "pscL"], writes=[("b", dc)])
                cbc_t = t_cbc if n % 2 == 0 else t_zz[:, 0:SS]
                cbc_key = "st1" if n % 2 == 0 else "zt"
                for dc in range(2):
                    em.op("scalar", lambda e, dc=dc, n=n: e.activation(
                        out=t_u[dc][:, :], in_=t_dt[dc][:, :], func=AF.Exp,
                        scale=t_aneg[:, NST * dc + n:NST * dc + n + 1]),
                        reads=[("dt", dc), aneg_key], writes=[("u", dc)])
                    em.op("vector", lambda e, dc=dc, n=n: e.tensor_tensor_scan(
                        out=t_h[:, :], data0=t_u[dc][:, :],
                        data1=t_b[dc][:, :],
                        initial=t_car[dc][:, n:n + 1],
                        op0=OP.mult, op1=OP.add),
                        reads=[("u", dc), ("b", dc), (car_key, dc)],
                        writes=["h"])
                    em.op("scalar", lambda e, dc=dc, n=n: e.copy(
                        out=t_car[dc][:, n:n + 1], in_=t_h[:, SS - 1:SS]),
                        reads=["h"], writes=[(car_key, dc)])
                    em.op("vector", lambda e, dc=dc, ct=cbc_t, ck=cbc_key:
                          e.tensor_tensor(
                              out=t_b[dc][:, :], in0=t_h[:, :],
                              in1=ct[:, :], op=OP.mult),
                          reads=["h", cbc_key], writes=[("b", dc)])
                    em.op("vector", lambda e, dc=dc: e.tensor_tensor(
                        out=t_y[dc][:, :], in0=t_y[dc][:, :],
                        in1=t_b[dc][:, :], op=OP.add),
                        reads=[("y", dc), ("b", dc)], writes=[("y", dc)])

            if _sub < 5:
                return
            ysrc = y1_src if scan_id == 0 else y3_src
            for dc in range(2):
                for k in range(4):
                    i = 4 * half + k
                    em.dma("gpsimd",
                           ysrc[g][256 * i + 128 * dc:
                                   256 * i + 128 * (dc + 1), :],
                           t_y[dc][:, 512 * k:512 * (k + 1)],
                           reads=[("y", dc)], writes=[("ysrc", scan_id, g)],
                           sname=f"ysnd{k % 2}{dc}")
            if half == 1:
                if scan_id == 0:
                    em.coll("AllToAll", OP.bypass, y1_src[g], y1_dst[g],
                            reads=[("ysrc", 0, g)], writes=[("y1d", g)])
                else:
                    em.coll("AllToAll", OP.bypass, y3_src[g], y3_dst[g],
                            reads=[("ysrc", 1, g)], writes=[("y3d", g)])

        def scanB_part1(s):
            g, half = s // 2, s % 2
            for k in range(4):
                i = 4 * half + k
                em.dma("sync", t_st3[0:1, 512 * k:512 * (k + 1)],
                       s3_dst[g][2 * i:2 * i + 1, :],
                       reads=[("s3d", g)], writes=["st1"])
                em.dma("sync", t_st3[32:33, 512 * k:512 * (k + 1)],
                       s3_dst[g][2 * i + 1:2 * i + 2, :],
                       reads=[("s3d", g)], writes=["st1"])
            for dc in range(2):
                src = x2_dst[g].rearrange("(i d) t -> d i t", i=NC)[
                    128 * dc:128 * (dc + 1), 4 * half:4 * half + 4, :]
                em.dma("sync",
                       t_z[dc][:, 4:4 + SS].rearrange(
                           "p (k t) -> p k t", k=4),
                       src, reads=[("x2d", g)], writes=["zt"])
                em.op("scalar", lambda e, dc=dc: e.copy(
                    out=t_z[dc][:, 1:4], in_=t_tail[dc][:, 1:4]),
                    reads=[("tail", dc)], writes=["zt"])
                ln_apply_seg(dc, t_z[dc][:, 4:4 + SS], t_z[dc][:, 4:4 + SS],
                             "zt", t_st3, "st1", "zt")
                em.op("scalar", lambda e, dc=dc: e.copy(
                    out=t_tail[dc][:, 1:4], in_=t_z[dc][:, 1 + SS:4 + SS]),
                    reads=["zt"], writes=[("tail", dc)])
                em.op("vector", lambda e, dc=dc: e.tensor_scalar(
                    out=t_u[dc][:, :], in0=t_z[dc][:, 1:1 + SS],
                    scalar1=t_cw[:, 4 * dc:4 * dc + 1],
                    scalar2=t_cb[:, dc:dc + 1], op0=OP.mult, op1=OP.add),
                    reads=["zt", "cw", "cb"], writes=[("u", dc)])
                for k in range(1, 4):
                    em.op("vector", lambda e, dc=dc, k=k:
                          e.scalar_tensor_tensor(
                              out=t_u[dc][:, :],
                              in0=t_z[dc][:, 1 + k:1 + k + SS],
                              scalar=t_cw[:, 4 * dc + k:4 * dc + k + 1],
                              in1=t_u[dc][:, :], op0=OP.mult, op1=OP.add),
                          reads=["zt", "cw", ("u", dc)],
                          writes=[("u", dc)])
                em.op("scalar", lambda e, dc=dc: e.copy(
                    out=t_ub[dc][:, :], in_=t_u[dc][:, :]),
                    reads=[("u", dc)], writes=[("ub", dc)])
            em.mm([(lambda e, q=q, dc=dc: e.matmul(
                PSC[:, 512 * q:512 * (q + 1)],
                t_wxb1[:, RB * dc:RB * (dc + 1)],
                t_ub[dc][:, 512 * q:512 * (q + 1)],
                start=(dc == 0), stop=(dc == 1)))
                for q in range(4) for dc in range(2)],
                reads=[("ub", 0), ("ub", 1), ("wxb1", 0), ("wxb1", 1)],
                writes=["pscL", "pscH"])
            em.op("scalar", lambda e: e.copy(out=t_sp[:, :],
                                             in_=PSC[:, :]),
                  reads=["pscL", "pscH"], writes=["sp"])
            em.dma("gpsimd", ar_src[s][0:128, :], t_sp[:, :],
                   reads=["sp"], writes=[("arsrc", s)])
            em.mm([(lambda e, q=q, dc=dc: e.matmul(
                PSC[0:48, 512 * q:512 * (q + 1)],
                t_wxb2[:, 48 * dc:48 * (dc + 1)],
                t_ub[dc][:, 512 * q:512 * (q + 1)],
                start=(dc == 0), stop=(dc == 1)))
                for q in range(4) for dc in range(2)],
                reads=[("ub", 0), ("ub", 1), ("wxb2", 0), ("wxb2", 1)],
                writes=["pscL", "pscH"])
            em.op("scalar", lambda e: e.copy(out=t_sp[0:48, :],
                                             in_=PSC[0:48, :]),
                  reads=["pscL", "pscH"], writes=["sp"])
            em.dma("gpsimd", ar_src[s][128:144, :], t_sp[0:16, :],
                   reads=["sp"], writes=[("arsrc", s)])
            em.dma("gpsimd", ar_src[s][144:160, :], t_sp[32:48, :],
                   reads=["sp"], writes=[("arsrc", s)])
            em.coll("AllReduce", OP.add, ar_src[s][0:160, :],
                    ar_dst[s][0:160, :],
                    reads=[("arsrc", s)], writes=[("ar", s)])

        _nseg = int(os.environ.get("K_NSEG", str(NSEG)))
        for s in range(_nseg):
            scan_sublayer(0, s)

        def ffn_phase(g, first, x_read_fn, ydst, ydst_key, xstash,
                      xout_base, xout_cols, xout_key, post_stats):
            stash_key = ("stash", g, first)
            for dc in range(NDC):
                par = dc % 2
                rk = x_read_fn(dc)
                em.dma("sync", t_y1r[par][:, :],
                       ydst[128 * dc:128 * (dc + 1), :],
                       reads=[ydst_key], writes=[("tyr", par)])
                em.op("vector", lambda e, par=par: e.tensor_tensor(
                    out=t_x[par][:, :], in0=t_xa[par][:, :],
                    in1=t_y1r[par][:, :],
                    op=OP.add), reads=rk + [("tyr", par)],
                    writes=[("tx", par)])
                em.dma("gpsimd", xstash[128 * dc:128 * (dc + 1), :],
                       t_x[par][:, :], reads=[("tx", par)],
                       writes=[stash_key])
                stat_mms(dc, [("tx", par)])
            ln_finalize("srow")
            em.mm([lambda e: e.matmul(PD[0][:, :], t_ones[0:1, :],
                                      t_srow[0:1, :],
                                      start=True, stop=True)],
                  reads=["srow", "ones"], writes=[("pd", 0)])
            em.mm([lambda e: e.matmul(PD[1][:, :], t_ones[32:33, :],
                                      t_srow[32:33, :],
                                      start=True, stop=True)],
                  reads=["srow", "ones"], writes=[("pd", 1)])
            for dc in range(NDC):
                par = dc % 2
                em.dma("sync", t_xa[par][:, :],
                       xstash[128 * dc:128 * (dc + 1), :],
                       reads=[stash_key], writes=[("txa", par)])
                em.op("vector", lambda e, par=par: e.tensor_tensor(
                    out=t_sg[par][:, :], in0=t_xa[par][:, :],
                    in1=PD[0][:, :],
                    op=OP.add), reads=[("txa", par), ("pd", 0)],
                    writes=[("tsg", par)])
                em.op("vector", lambda e, dc=dc, par=par: e.tensor_tensor(
                    out=t_z2[:, TBLK * dc:TBLK * (dc + 1)],
                    in0=t_sg[par][:, :],
                    in1=PD[1][:, :], op=OP.mult),
                    reads=[("tsg", par), ("pd", 1)], writes=[("z2", dc)])
            gp = g_stash[0] if first else g_stash[1]
            for hci in range(NHC):
                par = hci % 2
                pgb = PG if par == 0 else PD[0]
                pub = PU if par == 0 else PD[1]
                pgk = "pg" if par == 0 else ("pd", 0)
                puk = "pu" if par == 0 else ("pd", 1)
                em.dma("sync", t_wgu[par][:, 0:2048], wg_b[hci],
                       reads=[("wb", "g", j) for j in range(4)],
                       writes=[("wgu", par)])
                em.dma("sync", t_wgu[par][:, 2048:4096], wu_b[hci],
                       reads=[("wb", "u", j) for j in range(4)],
                       writes=[("wgu", par)])
                em.mm([(lambda e, dc=dc, par=par, pgb=pgb: e.matmul(
                    pgb[:, :], t_wgu[par][:, 128 * dc:128 * (dc + 1)],
                    t_z2[:, TBLK * dc:TBLK * (dc + 1)],
                    start=(dc == 0), stop=(dc == NDC - 1)))
                    for dc in range(NDC)],
                    reads=[("wgu", par)] + [("z2", dc) for dc in range(NDC)],
                    writes=[pgk])
                em.mm([(lambda e, dc=dc, par=par, pub=pub: e.matmul(
                    pub[:, :],
                    t_wgu[par][:, 2048 + 128 * dc:2048 + 128 * (dc + 1)],
                    t_z2[:, TBLK * dc:TBLK * (dc + 1)],
                    start=(dc == 0), stop=(dc == NDC - 1)))
                    for dc in range(NDC)],
                    reads=[("wgu", par)] + [("z2", dc) for dc in range(NDC)],
                    writes=[puk])
                em.op("scalar", lambda e, par=par, pgb=pgb: e.activation(
                    out=t_sg[par][:, :], in_=pgb[:, :], func=AF.Silu),
                    reads=[pgk], writes=[("tsg", par)])
                em.op("scalar", lambda e, par=par, pub=pub: e.copy(
                    out=t_su[par][:, :], in_=pub[:, :]),
                    reads=[puk], writes=[("tsu", par)])
                em.op("gpsimd", lambda e, par=par: e.tensor_tensor(
                    out=t_g[par][:, :], in0=t_sg[par][:, :],
                    in1=t_su[par][:, :],
                    op=OP.mult), reads=[("tsg", par), ("tsu", par)],
                    writes=[("tg", par)])
                em.dma("gpsimd", gp[:, TBLK * hci:TBLK * (hci + 1)],
                       t_g[par][:, :], reads=[("tg", par)],
                       writes=[("gst", hci)], sname=f"gstw{par}")
            BANKS = [(PSC[:, 512 * k:512 * (k + 1)],
                      "pscL" if k < 2 else "pscH") for k in range(4)] + \
                    [(PG, "pg"), (PU, "pu"), (PD[0], ("pd", 0)),
                     (PD[1], ("pd", 1))]
            for dcg in range(2):
                for q in range(16):
                    par = q % 2
                    em.dma("sync", t_wgu[par][:, :].rearrange(
                               "p (h c) -> p h c", h=4),
                           wd_b[4 * q:4 * q + 4, :,
                                1024 * dcg:1024 * (dcg + 1)].rearrange(
                               "h p c -> p h c"),
                           reads=[("wb", "d", j) for j in range(4)] +
                                 [("wduse", par)],
                           writes=[("wgu", par)])
                    em.dma("sync", t_gs[par][:, :],
                           gp[:, TBLK * 4 * q:TBLK * 4 * (q + 1)],
                           reads=[("gsdone", par)] +
                                 [("gst", 4 * q + i) for i in range(4)],
                           writes=[("gs", par)])
                    for dcl in range(8):
                        bank, bkey = BANKS[dcl]
                        em.mm([(lambda e, k=k, q=q, par=par, dcl=dcl,
                                bank=bank: e.matmul(
                                    bank[:, :],
                                    t_wgu[par][:, 1024 * k + 128 * dcl:
                                               1024 * k + 128 * (dcl + 1)],
                                    t_gs[par][:, TBLK * k:TBLK * (k + 1)],
                                    start=(q == 0 and k == 0),
                                    stop=(q == 15 and k == 3)))
                               for k in range(4)],
                              reads=[("wgu", par), ("gs", par)],
                              writes=[bkey, ("gsdone", par),
                                      ("wduse", par)])
                for dcl in range(8):
                    dc = 8 * dcg + dcl
                    par = dc % 2
                    bank, bkey = BANKS[dcl]
                    em.dma("sync", t_xa[par][:, :],
                           xstash[128 * dc:128 * (dc + 1), :],
                           reads=[stash_key], writes=[("txa", par)])
                    em.op("vector", lambda e, bank=bank, par=par:
                          e.scalar_tensor_tensor(
                              out=t_x[par][:, :], in0=bank[:, :], scalar=1.0,
                              in1=t_xa[par][:, :], op0=OP.mult, op1=OP.add),
                          reads=[bkey, ("txa", par)],
                          writes=[("tx", par)])
                    em.dma("gpsimd",
                           xout_base[128 * dc:128 * (dc + 1), xout_cols],
                           t_x[par][:, :], reads=[("tx", par)],
                           writes=[xout_key])
            if post_stats:
                for dc in range(NDC):
                    par = dc % 2
                    em.dma("sync", t_x[par][:, :],
                           xout_base[128 * dc:128 * (dc + 1), xout_cols],
                           reads=[xout_key], writes=[("tx", par)])
                    stat_mms(dc, [("tx", par)])
                ln_finalize("srow")
                em.dma("gpsimd", s3_src[g][0:1, :], t_srow[0:1, :],
                       reads=["srow"], writes=[("s3s", g)])
                em.dma("gpsimd", s3_src[g][1:2, :], t_srow[32:33, :],
                       reads=["srow"], writes=[("s3s", g)])
                em.coll("AllGather", OP.bypass, s3_src[g], s3_dst[g],
                        reads=[("s3s", g)], writes=[("s3d", g)])
                em.coll("AllToAll", OP.bypass, x2_src[g], x2_dst[g],
                        reads=[xout_key], writes=[("x2d", g)])

        if stage == 1:
            for g in range(NCH):
                tb = slice(TBLK * g, TBLK * (g + 1))
                have_y = ("y1d", g) in em.bw
                for dc in range(NDC):
                    par = dc % 2
                    em.dma("sync", t_xa[par][:, :],
                           x0_tok[128 * dc:128 * (dc + 1), tb],
                           writes=[("txa", par)])
                    if have_y:
                        em.dma("sync", t_y1r[par][:, :],
                               y1_dst[g][128 * dc:128 * (dc + 1), :],
                               reads=[("y1d", g)], writes=[("tyr", par)])
                        em.op("vector", lambda e, par=par: e.tensor_tensor(
                            out=t_x[par][:, :], in0=t_xa[par][:, :],
                            in1=t_y1r[par][:, :],
                            op=OP.add), reads=[("txa", par), ("tyr", par)],
                            writes=[("tx", par)])
                    else:
                        em.op("vector", lambda e, par=par: e.tensor_copy(
                            out=t_x[par][:, :], in_=t_xa[par][:, :]),
                            reads=[("txa", par)], writes=[("tx", par)])
                    em.dma("sync", y_out[128 * dc:128 * (dc + 1), tb],
                           t_x[par][:, :], reads=[("tx", par)],
                           writes=["out"])
        else:
            _nch = int(os.environ.get("K_NCH", str(NCH)))
            for g in range(_nch):
                tb = slice(TBLK * g, TBLK * (g + 1))

                def read_x0(dc, tb=tb):
                    par = dc % 2
                    em.dma("sync", t_xa[par][:, :],
                           x0_tok[128 * dc:128 * (dc + 1), tb],
                           writes=[("txa", par)])
                    return [("txa", par)]
                ffn_phase(g, True, read_x0, y1_dst[g], ("y1d", g),
                          x1_stash[g], x2_src[g], slice(0, TBLK),
                          ("xout", g, 1), post_stats=(stage >= 3))
                if stage == 2 and ("xout", g, 1) in em.bw:
                    for dc in range(NDC):
                        par = dc % 2
                        em.dma("sync", t_x[par][:, :],
                               x2_src[g][128 * dc:128 * (dc + 1), :],
                               reads=[("xout", g, 1)], writes=[("tx", par)])
                        em.dma("sync", y_out[128 * dc:128 * (dc + 1), tb],
                               t_x[par][:, :], reads=[("tx", par)],
                               writes=["out"])
            if stage >= 3:
                _nsegb = int(os.environ.get("K_NSEGB", str(NSEG)))
                for s in range(_nsegb):
                    scanB_part1(s)
                    scan_sublayer(1, s)
                for g in range(_nch):
                    tb = slice(TBLK * g, TBLK * (g + 1))
                    if stage == 3:
                        for dc in range(NDC):
                            par = dc % 2
                            em.dma("sync", t_xa[par][:, :],
                                   x2_src[g][128 * dc:128 * (dc + 1), :],
                                   reads=[("xout", g, 1)],
                                   writes=[("txa", par)])
                            em.dma("sync", t_y1r[par][:, :],
                                   y3_dst[g][128 * dc:128 * (dc + 1), :],
                                   reads=[("y3d", g)], writes=[("tyr", par)])
                            em.op("vector", lambda e, par=par: e.tensor_tensor(
                                out=t_x[par][:, :], in0=t_xa[par][:, :],
                                in1=t_y1r[par][:, :], op=OP.add),
                                reads=[("txa", par), ("tyr", par)],
                                writes=[("tx", par)])
                            em.dma("sync",
                                   y_out[128 * dc:128 * (dc + 1), tb],
                                   t_x[par][:, :], reads=[("tx", par)],
                                   writes=["out"])
                    else:
                        def read_x2(dc, g=g):
                            par = dc % 2
                            em.dma("sync", t_xa[par][:, :],
                                   x2_src[g][128 * dc:128 * (dc + 1), :],
                                   reads=[("xout", g, 1)],
                                   writes=[("txa", par)])
                            return [("txa", par)]
                        ffn_phase(g, False, read_x2, y3_dst[g], ("y3d", g),
                                  x3_stash[g], y_out, tb,
                                  ("out", g), post_stats=False)

        if "y_out" in em.dsems:
            osem, ocnt = em.dsems["y_out"][0], em.dsems["y_out"][1]
            em.streams["sync"].append(
                lambda e, s=osem, c=ocnt: e.wait_ge(s, c))

        with nc.Block() as block:
            em.finish(block)

    return nc


def _prep_inputs(inputs):
    x0 = np.asarray(inputs["L0_attn_a_x"])[0]
    x0t = np.ascontiguousarray(x0.T)
    wxa = np.asarray(inputs["L0_attn_a_W_xproj"])
    wdt = np.asarray(inputs["L0_attn_a_W_dt"])
    bdt = np.asarray(inputs["L0_attn_a_b_dt"])
    alog = np.asarray(inputs["L0_attn_a_A_log"])
    dparam = np.asarray(inputs["L0_attn_a_D_param"])
    cw = np.asarray(inputs["L0_attn_b_conv_w"])[:, 0, :]
    cb = np.asarray(inputs["L0_attn_b_conv_b"])
    wxb = np.asarray(inputs["L0_attn_b_xproj_w"])
    wdtb = np.asarray(inputs["L0_attn_b_dtproj_w"])
    bdtb = np.asarray(inputs["L0_attn_b_dtproj_b"])
    alogb = np.asarray(inputs["L0_attn_b_A_log"])
    dskip = np.asarray(inputs["L0_attn_b_D_skip"])
    wg = np.asarray(inputs["L0_ffn_w_gate"])
    wu = np.asarray(inputs["L0_ffn_w_up"])
    wd = np.asarray(inputs["L0_ffn_w_down"])
    f32 = np.float32

    def pack48(w, r0):
        out = np.zeros((D, 48), f32)
        out[:, 0:16] = w[r0:r0 + 16].T
        out[:, 32:48] = w[r0 + 16:r0 + 32].T
        return out

    wxa1_h = np.ascontiguousarray(wxa[0:64].T)
    wxa2_h = pack48(wxa, 64)
    wxb1_full = np.ascontiguousarray(wxb[0:128].T)
    wxb2_full = pack48(wxb, 128)

    sel_h = np.zeros((NST, NST * 128), f32)
    for n in range(NST):
        sel_h[n, 128 * n:128 * (n + 1)] = 1.0
    ones_h = np.ones((1, 128), f32)

    wg_h = np.ascontiguousarray(
        wg.T.reshape(NDC, 128, NHC, 128).transpose(2, 1, 0, 3)
        .reshape(NHC, 128, 2048))
    wu_h = np.ascontiguousarray(
        wu.T.reshape(NDC, 128, NHC, 128).transpose(2, 1, 0, 3)
        .reshape(NHC, 128, 2048))
    wd_h = np.ascontiguousarray(wd.T.reshape(NHC, 128, D))

    def dsh2(v):
        return np.ascontiguousarray(v.reshape(2, 128).T)

    def dsh2n(v):
        return np.ascontiguousarray(
            v.reshape(2, 128, NST).transpose(1, 0, 2).reshape(128, 2 * NST))

    in_maps = []
    for j in range(NC):
        ds = slice(DL * j, DL * (j + 1))
        blocks = [x0t[:, TBLK * (8 * c + j):TBLK * (8 * c + j + 1)]
                  for c in range(NCH)]
        m = {
            "x0_tok": np.ascontiguousarray(np.concatenate(blocks, axis=1)),
            "x0_dsh": np.ascontiguousarray(x0t[ds, :]),
            "wxa1": wxa1_h, "wxa2": wxa2_h,
            "wxb1": np.ascontiguousarray(wxb1_full[ds, :]),
            "wxb2": np.ascontiguousarray(wxb2_full[ds, :]),
            "wdta": np.ascontiguousarray(wdt[ds, :].T),
            "wdtb": np.ascontiguousarray(wdtb[ds, :].T),
            "bdta": dsh2(bdt[ds]),
            "bdtb": dsh2(bdtb[ds]),
            "aloga": dsh2n(alog[ds, :]),
            "alogb": dsh2n(alogb[ds, :]),
            "dpar": dsh2(dparam[ds]),
            "dskp": dsh2(dskip[ds]),
            "cwin": np.ascontiguousarray(
                cw[ds, :].reshape(2, 128, 4).transpose(1, 0, 2)
                .reshape(128, 8)),
            "cbin": dsh2(cb[ds]),
            "selin": sel_h, "onesin": ones_h,
            "wg_s": wg_h, "wu_s": wu_h, "wd_s": wd_h,
        }
        in_maps.append(m)
    return in_maps


def _assemble(results):
    out = np.empty((L, D), np.float32)
    for j in range(NC):
        y = results[j]["y_out"]
        for c in range(NCH):
            b = 8 * c + j
            out[TBLK * b:TBLK * (b + 1), :] = y[:, TBLK * c:TBLK * (c + 1)].T
    return out.reshape(1, L, D)


def run(inputs, stage=4):
    in_maps = _prep_inputs(inputs)
    nc = build_nc(stage)
    res = run_bass_kernel_spmd(nc, in_maps, core_ids=list(range(NC)))
    return _assemble(res.results)


def kernel(**inputs):
    return run(inputs, stage=4)
